# version 18
# speedup vs baseline: 1.0208x; 1.0065x over previous
import os
import contextlib
import numpy as np
import concourse.bass as bass
import concourse.mybir as mybir
from concourse.bass_utils import run_bass_kernel_spmd

F32 = mybir.dt.float32
BF16 = mybir.dt.bfloat16
AF = mybir.ActivationFunctionType
ALU = mybir.AluOpType
AX = mybir.AxisListType

B, T, D = 2, 8192, 1024
W = 512
OWN = 2048
XH = 1664
XSPAN = XH + OWN
NT = XSPAN // 128
K1OFF, M1OFF, K2OFF, Q2OFF = 128, 640, 1152, 1664
K1SPAN = XSPAN - K1OFF
M1SPAN = XSPAN - M1OFF
K2SPAN = XSPAN - K2OFF
HEADS, HD = 8, 64
HID = 2730
NC1 = 22
LASTC = HID - 21 * 128
EXP_BIAS = 0.0
QK_SCALE = 8.0
NEG = -1e30
MAGIC = 12582912.0
C1 = 6.28125
C2 = float(2.0 * np.pi - C1)
PIH = float(np.pi / 2)
PI = float(np.pi)
INV2PI = float(1.0 / (2.0 * np.pi))
RC = 1024
STAGE = int(os.environ.get("K_STAGE", "99"))
TAPS = [s for s in os.environ.get("K_TAPS", "").split(",") if s]
WALL = os.environ.get("K_WALL", "1") == "1"
WALL_ENG = set((os.environ.get(
    "K_WALLENG", "sync,vector,scalar,tensor,gpsimd")
    if WALL else "").split(","))
FREE = set(os.environ.get("K_FREE", "mlp,qkv,attn,rwkv").split(","))
ARENA_N = 20480


class VEGuard:

    _OPS = {"tensor_tensor", "tensor_scalar", "tensor_copy", "tensor_add",
            "tensor_sub", "tensor_mul", "tensor_max", "tensor_scalar_add",
            "tensor_scalar_sub", "tensor_scalar_mul", "tensor_scalar_max",
            "tensor_scalar_min", "scalar_tensor_tensor", "tensor_tensor_scan",
            "reduce_sum", "reduce_max", "tensor_reduce", "select",
            "reciprocal", "memset", "memzero"}

    def __init__(self, kb, eng):
        self._kb, self._e = kb, eng
        self._hist = []

    @staticmethod
    def _span(ap):
        try:
            dt_size = 2 if ap.dtype == BF16 else 4
            lo = int(ap.offset) * dt_size
            hi = lo + int(ap.size()) * dt_size * 2
            return (ap.tensor.name, lo, hi)
        except Exception:
            return (getattr(getattr(ap, "tensor", None), "name", "?"), 0, 1 << 40)

    @staticmethod
    def _is_ap(x):
        return hasattr(x, "tensor") and hasattr(x, "offset")

    def _overlap(self, ins):
        for nm, lo, hi in self._hist:
            for ap in ins:
                s = self._span(ap)
                if s[0] == nm and s[1] < hi and lo < s[2]:
                    return True
        return False

    def __getattr__(self, name):
        raw = getattr(self._e, name)
        if name not in self._OPS:
            return raw

        def call(*args, **kw):
            aps = [a for a in list(args) + list(kw.values()) if self._is_ap(a)]
            outs, ins = aps[:1], aps[1:]
            if "accum_out" in kw and self._is_ap(kw["accum_out"]):
                outs = outs + [kw["accum_out"]]
            if self._overlap(ins):
                j = self._kb.junk
                self._e.tensor_scalar(j[:, 64:128], j[:, 0:64], 1.0, None,
                                      ALU.mult)
                self._hist = []
            instr = raw(*args, **kw)
            wide = all(int(a.size()) // max(1, int(a.partition_size())) >= 64
                       for a in outs) and name not in (
                "reduce_sum", "reduce_max", "tensor_reduce", "reciprocal",
                "tensor_tensor_scan")
            if wide:
                self._hist = []
            else:
                self._hist = (self._hist + [self._span(a) for a in outs])[-4:]
            return instr
        return call


class WrapBlock:

    def __init__(self, kb, block):
        self._kb, self._b = kb, block

    def _wrap(self, deco, guard, name):
        ff = self._kb.freeflow

        def outer(fn):
            def inner(eng):
                if name in WALL_ENG and not ff:
                    self._kb.wall(eng)
                return fn(VEGuard(self._kb, eng) if guard else eng)
            return deco(inner)
        return outer

    def __getattr__(self, name):
        return self._wrap(getattr(self._b, name), name == "vector", name)


class KB:
    def __init__(self, nc, block, ctx, names):
        self.freeflow = False
        self.nc, self.block = nc, WrapBlock(self, block)
        self.h, self.c, self.state = {}, {}, {}
        for n in names:
            self.h[n] = ctx.enter_context(nc.semaphore(n))
            self.c[n] = 0

    def inc(self, instr, name, n=1):
        self.c[name] += n
        instr.then_inc(self.h[name], n)
        return self.c[name]

    def snap(self):
        return dict(self.c)

    def w(self, eng, snap, *names):
        for n in names:
            if snap.get(n, 0):
                eng.wait_ge(self.h[n], snap[n])

    def sget(self, key, dfl=None):
        return self.state.get(key, dfl or {})

    def sp(self, v):
        v.tensor_scalar(self.junk[:, 64:128], self.junk[:, 0:64], 1.0, None,
                        ALU.mult)

    def wall(self, eng):
        for n in self.h:
            if self.c[n]:
                eng.wait_ge(self.h[n], self.c[n])


def bc3(t2d, n, f):
    return t2d.unsqueeze(1).broadcast_to([128, n, f])


class Arena:

    def __init__(self, tensor):
        self.t = tensor
        self.off = 0

    def reset(self):
        self.off = 0

    def f32(self, n, *shape):
        v = self.t[:, self.off:self.off + n]
        self.off += n
        assert self.off <= ARENA_N, f"arena overflow {self.off}"
        if shape:
            v = v.rearrange("p (" + " ".join(chr(97 + i) for i in range(len(shape)))
                            + ") -> p " + " ".join(chr(97 + i) for i in range(len(shape))),
                            **{chr(97 + i): s for i, s in enumerate(shape)})
        return v

    def bf16(self, n_bf, *shape):
        assert n_bf % 2 == 0
        v = self.t[:, self.off:self.off + n_bf // 2].bitcast(BF16)
        self.off += n_bf // 2
        assert self.off <= ARENA_N, f"arena overflow {self.off}"
        if shape:
            v = v.rearrange("p (" + " ".join(chr(97 + i) for i in range(len(shape)))
                            + ") -> p " + " ".join(chr(97 + i) for i in range(len(shape))),
                            **{chr(97 + i): s for i, s in enumerate(shape)})
        return v


def emit_prologue(kb, t, d):
    block, sem = kb.block, kb

    @block.sync
    def _(s):
        sem.inc(s.dma_start(out=t.consts[:], in_=d.consts), "LD", 16)
        for o, i in [(t.mb_sb, d.maskbias), (t.goff_sb, d.goff),
                     (t.wcol, d.wcol_i), (t.ucol, d.ucol_i),
                     (t.bkcol, d.bkcol_i), (t.bvcol, d.bvcol_i),
                     (t.alpha_sb, d.alpha_in),
                     (t.maska_sb, d.maskadd), (t.selx_sb, d.selx)]:
            sem.inc(s.dma_start(out=o[:], in_=i), "LD", 16)

    @block.gpsimd
    def _(g):
        for o, i in [(t.ident, d.ident_i), (t.trile, d.trile_i),
                     (t.trige, d.trige_i)]:
            sem.inc(g.dma_start(out=o[:], in_=i), "GW", 16)
        sem.inc(g.dma_start(out=t.identf[:], in_=d.ident_i), "GW", 16)
        g.iota(t.iota29[:], pattern=[[128, NT]], base=0, channel_multiplier=1,
               allow_small_or_imprecise_dtypes=True)
        sem.inc(g.iota(t.halfidx[:], pattern=[[0, 2], [2, 32]], base=0,
                       channel_multiplier=0,
                       allow_small_or_imprecise_dtypes=True), "GP", 1)
    snap = sem.snap()

    @block.vector
    def _(v):
        kb.w(v, snap, "GP", "LD")
        v.memset(t.junk[:], 1.0)
        v.memset(t.onescol[:], 1.0)
        v.memset(t.onescolf[:], 1.0)
        v.memset(t.zcol[:], 0.0)
        v.memset(t.vtm[:].rearrange("p t (h c) -> p t h c", h=8, c=65)
                 [:, :, :, 64:65], 1.0)
        v.tensor_scalar(t.lnbase[:], t.halfidx[:], 25.6, 1.0 / 89.6,
                        ALU.add, ALU.mult)
        sem.inc(v.tensor_tensor(t.tpos[:], t.iota29[:],
                                t.goff_sb[:, 0:1].broadcast_to([128, NT]),
                                ALU.add), "VE", 1)
    snap2 = sem.snap()

    @block.scalar
    def _(sc):
        kb.w(sc, snap2, "VE")
        sc.activation(t.lnbase[:], t.lnbase[:], AF.Ln)
        sem.inc(sc.activation(t.invfreq[:], t.halfidx[:], AF.Exp,
                              scale=float(-np.log(10000.0) / HD)), "SC", 1)
    kb.state["prologue"] = sem.snap()


def emit_tables(kb, t, d, li, a):
    block, sem = kb.block, kb
    tCQ, tSQ, tCK, tSK = a["tCQ"], a["tSQ"], a["tCK"], a["tSK"]
    s1, s2 = a["tscr"], a["tscr2"]

    @block.sync
    def _(s):
        for o, i in [(t.qs_sb, d.qs_i[li]), (t.qsp_sb, d.qsp_i[li]),
                     (t.ks_sb, d.ks_i[li]), (t.ksp_sb, d.ksp_i[li])]:
            sem.inc(s.dma_start(out=o[:], in_=i), "LD", 16)
    snap = sem.snap()
    pro = kb.state["prologue"]

    @block.vector
    def _(v):
        kb.w(v, snap, "LD")
        kb.w(v, pro, "SC", "VE")
        kb.w(v, kb.sget("arena_free"), "VE", "SC", "PE", "GP", "ST", "GW")
        v.tensor_tensor(s1[:], t.tpos[:].unsqueeze(2).broadcast_to(
            [128, NT, HD]), bc3(t.invfreq[:], NT, HD), ALU.mult)
        for dst, shift in ((tSQ, 0.0), (tCQ, PIH)):
            src = s1
            if shift:
                v.tensor_scalar(s2[:], s1[:], shift, None, ALU.add)
                src = s2
            v.tensor_scalar(tCK[:], src[:], INV2PI, MAGIC, ALU.mult, ALU.add)
            v.tensor_scalar(tCK[:], tCK[:], MAGIC, None, ALU.subtract)
            v.scalar_tensor_tensor(tSK[:], tCK[:], -C1, src[:], ALU.mult,
                                   ALU.add)
            v.scalar_tensor_tensor(tSK[:], tCK[:], -C2, tSK[:], ALU.mult,
                                   ALU.add)
            sem.inc(v.tensor_scalar(dst[:], tSK[:], PI, -PI, ALU.min, ALU.max),
                    "VE", 1)
    snap2 = sem.snap()

    @block.scalar
    def _(sc):
        kb.w(sc, snap2, "VE")
        sc.activation(tSQ[:], tSQ[:], AF.Sin)
        sem.inc(sc.activation(tCQ[:], tCQ[:], AF.Sin), "SC", 1)
    snap3 = sem.snap()

    @block.vector
    def _(v):
        kb.w(v, snap3, "SC")
        v.tensor_scalar(t.tpw[:], t.tpos[:], -4096.0, 1.0 / 256.0,
                        ALU.add, ALU.mult)
        sem.inc(v.tensor_tensor(
            s2[:], t.tpw[:].unsqueeze(2).broadcast_to([128, NT, HD]),
            bc3(t.lnbase[:], NT, HD), ALU.mult), "VE", 1)
    snap4 = sem.snap()

    @block.scalar
    def _(sc):
        kb.w(sc, snap4, "VE")
        sc.activation(s1[:], s2[:], AF.Exp)
        sem.inc(sc.activation(s2[:], s2[:], AF.Exp, scale=-1.0), "SC", 1)
    snap5 = sem.snap()

    @block.vector
    def _(v):
        kb.w(v, snap5, "SC")
        v.tensor_tensor(tCK[:], tCQ[:], s2[:], ALU.mult)
        v.tensor_tensor(tCK[:], tCK[:], bc3(t.ks_sb[:], NT, HD), ALU.mult)
        v.tensor_tensor(tSK[:], tSQ[:], s2[:], ALU.mult)
        v.tensor_tensor(tSK[:], tSK[:], bc3(t.ksp_sb[:], NT, HD), ALU.mult)
        v.tensor_scalar(tSK[:, :, 0:32], tSK[:, :, 0:32], -1.0, None, ALU.mult)
        v.tensor_tensor(tCQ[:], tCQ[:], s1[:], ALU.mult)
        v.tensor_tensor(tCQ[:], tCQ[:], bc3(t.qs_sb[:], NT, HD), ALU.mult)
        v.tensor_tensor(tSQ[:], tSQ[:], s1[:], ALU.mult)
        v.tensor_tensor(tSQ[:], tSQ[:], bc3(t.qsp_sb[:], NT, HD), ALU.mult)
        sem.inc(v.tensor_scalar(tSQ[:, :, 0:32], tSQ[:, :, 0:32], -1.0, None,
                                ALU.mult), "VE", 1)
    kb.state["tables"] = sem.snap()


def emit_ln_tile(kb, t, a, tt, slot, dst_writer):
    block, sem = kb.block, kb
    xt = a["xtile"][slot]

    @block.sync
    def _(s):
        kb.w(s, kb.sget("xtile_free"), "SC", "VE")
        sem.inc(s.dma_start(out=xt, in_=kb.d.xs[128 * tt:128 * tt + 128, :]),
                "LD", 16)
    snap = sem.snap()

    @block.vector
    def _(v):
        kb.w(v, snap, "LD")
        v.reduce_sum(t.stat[:, 0:1], xt, AX.X)
        kb.sp(v)
        sem.inc(v.tensor_scalar(t.stat[:, 0:1], t.stat[:, 0:1], 1.0 / D, None,
                                ALU.mult), "VE", 1)

    sc0 = sem.c["SC"]
    ve0 = sem.c["VE"]

    @block.scalar
    def _(sc):
        kb.w(sc, snap, "LD")
        kb.w(sc, kb.sget("xsq_free"), "VE")
        sem.inc(sc.activation(a["xsq"], xt, AF.Square), "SC", 1)
        sc.wait_ge(kb.h["VE"], ve0 + 1)
        sc.activation(t.stat[:, 3:4], t.stat[:, 1:2], AF.Ln, bias=1e-5)
        sem.inc(sc.activation(t.stat[:, 3:4], t.stat[:, 3:4], AF.Exp,
                              scale=-0.5), "SC", 1)

    _ff = kb.freeflow
    kb.freeflow = True

    @block.vector
    def _(v):
        v.wait_ge(kb.h["SC"], sc0 + 1)
        v.reduce_sum(t.stat[:, 1:2], a["xsq"], AX.X)
        kb.sp(v)
        v.tensor_scalar(t.stat[:, 1:2], t.stat[:, 1:2], 1.0 / D, None, ALU.mult)
        v.tensor_tensor(t.stat[:, 2:3], t.stat[:, 0:1], t.stat[:, 0:1], ALU.mult)
        sem.inc(v.tensor_tensor(t.stat[:, 1:2], t.stat[:, 1:2], t.stat[:, 2:3],
                                ALU.subtract), "VE", 1)
    kb.freeflow = _ff
    kb.state["xsq_free"] = sem.snap()
    snap5 = sem.snap()

    @block.vector
    def _(v):
        kb.w(v, snap5, "SC")
        kb.w(v, kb.sget("xntm_free"), "PE")
        v.tensor_tensor(t.stat[:, 4:5], t.stat[:, 0:1], t.stat[:, 3:4], ALU.mult)
        v.tensor_scalar(t.stat[:, 4:5], t.stat[:, 4:5], -1.0, None, ALU.mult)
        sem.inc(v.tensor_scalar(a["xntm"], xt, t.stat[:, 3:4], t.stat[:, 4:5],
                                ALU.mult, ALU.add), "VE", 1)
    snap6 = sem.snap()
    kb.state["xtile_free"] = sem.snap()

    @block.tensor
    def _(tr):
        kb.w(tr, snap6, "VE")
        ev = {}
        for c in range(8):
            if c >= 2:
                tr.wait_ge(kb.h["SC"], ev[c - 2])
            else:
                kb.w(tr, kb.sget("pstp_free"), "SC")
            sem.inc(tr.transpose(t.ps_tp[:, c % 2],
                                 a["xntm"][:, 128 * c:128 * c + 128],
                                 t.ident[:]), "PE", 1)
            pe_at = sem.c["PE"]

            @block.scalar
            def _(sc, c=c, pe_at=pe_at):
                sc.wait_ge(kb.h["PE"], pe_at)
                sem.inc(dst_writer(sc, c, t.ps_tp[:, c % 2]), "SC", 1)
            ev[c] = sem.c["SC"]
    kb.state["xntm_free"] = sem.snap()
    kb.state["pstp_free"] = sem.snap()


def qkv_arena(kb, li):
    ar = kb.ar
    ar.reset()
    a = {}
    a["tCQ"] = ar.f32(NT * HD, NT, HD)
    a["tSQ"] = ar.f32(NT * HD, NT, HD)
    a["tCK"] = ar.f32(NT * HD, NT, HD)
    a["tSK"] = ar.f32(NT * HD, NT, HD)
    base = ar.off
    a["tscr"] = ar.f32(NT * HD, NT, HD)
    a["tscr2"] = ar.f32(NT * HD, NT, HD)
    ar.off = base
    if li == 0:
        xt = ar.f32(1024)
        a["xtile"] = [xt, xt]
        a["xsq"] = ar.f32(1024)
        a["xntm"] = ar.bf16(1024)
        a["sqscr"] = [a["xsq"][:, 0:512], a["xsq"][:, 512:1024]]
    else:
        a["x2roll"] = [ar.bf16(8 * 130, 8, 130) for _ in range(2)]
        a["sqscr"] = [ar.f32(512) for _ in range(2)]
    a["qkf"] = [ar.f32(512) for _ in range(2)]
    a["rot2"] = [ar.f32(512) for _ in range(2)]
    a["rotmp"] = [ar.f32(512) for _ in range(2)]
    a["wqkv"] = ar.bf16(8 * 1536, 8, 1536)
    a["krot"] = [ar.bf16(512) for _ in range(2)]
    return a


def emit_qkv(kb, t, d, li):
    block, sem = kb.block, kb
    a = qkv_arena(kb, li)
    emit_tables(kb, t, d, li, a)

    @block.gpsimd
    def _(g):
        kb.w(g, kb.sget("arena_free"), "PE", "VE", "SC")
        sem.inc(g.dma_start(out=a["wqkv"], in_=d.wqkv_i[li].rearrange(
            "(c p) n -> p c n", p=128)), "GW", 16)
    kb.state["wq"] = sem.snap()

    tiles = list(range(1, NT)) if li == 0 else list(range(K2OFF // 128, NT))
    koff_t0 = tiles[0]
    maxt = int(os.environ.get("K_MAXT", "99"))
    tiles = tiles[:maxt]
    koff_t = koff_t0

    if li == 0:
        def mk_writer(tt):
            def wr(sc, c, ps):
                if tt >= 13:
                    return sc.copy(t.xn_own[:, c, 1 + 128 * (tt - 13):
                                            129 + 128 * (tt - 13)], ps)
                return sc.copy(t.xn_roll[tt % 3][:, c, 1:129], ps)
            return wr
        emit_ln_tile(kb, t, a, 0, 0, mk_writer(0))
        bs = sem.snap()

        @block.vector
        def _(v):
            kb.w(v, bs, "SC")
            sem.inc(v.memset(t.xn_roll[0][:, :, 0:1], 0.0), "VE", 1)
        kb.state["xnb_0"] = sem.snap()

    for tt in tiles:
        if li == 0:
            emit_ln_tile(kb, t, a, tt, tt % 2, mk_writer(tt))
            bs = sem.snap()

            @block.vector
            def _(v, tt=tt):
                kb.w(v, bs, "SC")
                if tt <= 12:
                    ins = v.tensor_copy(t.xn_roll[tt % 3][:, :, 0:1],
                                        t.xn_roll[(tt - 1) % 3][:, :, 128:129])
                elif tt == 13:
                    ins = v.tensor_copy(t.xn_own[:, :, 0:1],
                                        t.xn_roll[0][:, :, 128:129])
                else:
                    ins = v.engine_nop()
                sem.inc(ins, "VE", 1)
            kb.state[f"xnb_{tt}"] = sem.snap()

            def lhs(c, shifted, tt=tt):
                if tt >= 13:
                    b0 = 128 * (tt - 13) + (0 if shifted else 1)
                    return t.xn_own[:, c, b0:b0 + 128]
                b0 = 0 if shifted else 1
                return t.xn_roll[tt % 3][:, c, b0:b0 + 128]
            dep = kb.state[f"xnb_{tt}"]
        else:
            rl = a["x2roll"][tt % 2]

            @block.gpsimd
            def _(g, tt=tt, rl=rl):
                kb.w(g, kb.sget(f"x2roll_free_{tt % 2}"), "PE")
                sem.inc(g.dma_start(out=rl, in_=kb.d.x2_d.ap()
                                    .transpose([1, 0, 2])
                                    [:, :, 128 * tt - 640:128 * tt - 640 + 130]),
                        "GW", 16)
            dep = sem.snap()

            def lhs(c, shifted, tt=tt, rl=rl):
                b0 = 1 - (1 if shifted else 0)
                return rl[:, c, b0:b0 + 128]
        _qkv_tile(kb, t, d, li, tt, koff_t, lhs, dep, a)
        if li == 1:
            kb.state[f"x2roll_free_{tt % 2}"] = sem.snap()
    kb.state[f"qkv_done{li}"] = sem.snap()
    kb.state["arena_free"] = sem.snap()


def _qkv_tile(kb, t, d, li, tt, koff_t, lhs, dep, a):
    block, sem = kb.block, kb
    kb.freeflow = "qkv" in FREE
    sub = int(os.environ.get("K_SUB", "9"))
    if sub < 3:
        return
    kvt = tt - koff_t
    tbl = kb.state["tables"]

    @block.tensor
    def _(tr):
        bar = os.environ.get("K_BAR", "")
        if bar == "all":
            kb.wall(tr)
        elif bar:
            for nm in bar.split(","):
                if kb.c[nm]:
                    tr.wait_ge(kb.h[nm], kb.c[nm])
        kb.w(tr, kb.state["wq"], "GW")
        kb.w(tr, dep, "VE", "SC", "GW")
        kb.w(tr, kb.sget("qkv_ps_free"), "SC", "VE")
        nsi = int(os.environ.get("K_NSI", "3"))
        for si, lo in enumerate((0, 512, 1024)[:nsi]):
            for c in range(8):
                sem.inc(tr.matmul(t.ps_qkv[si], lhs(c, c >= 4),
                                  a["wqkv"][:, c, lo:lo + 512],
                                  start=(c == 0), stop=(c == 7)), "PE", 1)
            kb.state[f"qkv_mm_{si}"] = sem.snap()
        for si in range(nsi, 3):
            kb.state[f"qkv_mm_{si}"] = sem.snap()

    if sub < 4:
        return
    mmv = kb.state["qkv_mm_2"]

    @block.vector
    def _(v):
        kb.w(v, mmv, "PE")
        sem.inc(v.tensor_copy(
            t.vtm[:, kvt].rearrange("p (h c) -> p h c", h=8, c=65)[:, :, 0:64],
            t.ps_qkv[2].rearrange("p (h c) -> p h c", h=8, c=64)), "VE", 1)

    if sub < 5:
        return
    for si in (0, 1):
        mm = kb.state[f"qkv_mm_{si}"]
        sb = si
        has_tp = not (si == 0 and kvt < 4)
        qcol = 128 * kvt - (512 if si == 0 else 0)
        sc0 = sem.c["SC"]
        ve0 = sem.c["VE"]
        pe0 = sem.c["PE"]
        qkf_pre = kb.sget(f"qkf_free_{sb}")
        krot_pre = kb.sget(f"krot_free_{sb}")
        pstp_pre = kb.sget("pstp_free")
        CT, ST = (a["tCQ"], a["tSQ"]) if si == 0 else (a["tCK"], a["tSK"])
        dstt = t.qfm if si == 0 else t.kfm

        @block.scalar
        def _(sc, si=si, sb=sb, mm=mm, has_tp=has_tp, qcol=qcol,
              sc0=sc0, ve0=ve0, pe0=pe0, dstt=dstt, qkf_pre=qkf_pre):
            kb.w(sc, mm, "PE")
            kb.w(sc, qkf_pre, "VE")
            sc.copy(a["qkf"][sb], t.ps_qkv[si])
            sem.inc(sc.activation(a["sqscr"][sb], t.ps_qkv[si], AF.Square),
                    "SC", 1)
            sc.wait_ge(kb.h["VE"], ve0 + 1)
            sc.activation(t.rsqt[sb][:], t.rsqt[sb][:], AF.Ln, bias=1e-24)
            sem.inc(sc.activation(t.rsqt[sb][:], t.rsqt[sb][:], AF.Exp,
                                  scale=-0.5), "SC", 1)
            if has_tp:
                for pr in range(4):
                    sc.wait_ge(kb.h["PE"], pe0 + pr + 1)
                    sem.inc(sc.copy(dstt[:, pr, qcol:qcol + 128],
                                    t.ps_tp[:, pr % 2]), "SC", 1)

        @block.vector
        def _(v, sb=sb, sc0=sc0):
            v.wait_ge(kb.h["SC"], sc0 + 1)
            v.reduce_sum(
                t.rsqt[sb][:], a["sqscr"][sb].rearrange(
                    "p (h c) -> p h c", h=8, c=64), AX.X)
            kb.sp(v)
            sem.inc(v.engine_nop(), "VE", 1)

        @block.vector
        def _(v, sb=sb, sc0=sc0, CT=CT, ST=ST, tt=tt, krot_pre=krot_pre):
            v.wait_ge(kb.h["SC"], sc0 + 2)
            kb.w(v, tbl, "VE")
            kb.w(v, krot_pre, "PE")
            qv = a["qkf"][sb].rearrange("p (h u c) -> p h u c", h=8, u=2, c=32)
            tv = a["rotmp"][sb].rearrange("p (h u c) -> p h u c", h=8, u=2, c=32)
            st0 = ST[:, tt, 0:32].unsqueeze(1).broadcast_to([128, 8, 32])
            st1 = ST[:, tt, 32:64].unsqueeze(1).broadcast_to([128, 8, 32])
            v.tensor_tensor(tv[:, :, 0], qv[:, :, 1], st0, ALU.mult)
            v.tensor_tensor(tv[:, :, 1], qv[:, :, 0], st1, ALU.mult)
            ct = CT[:, tt].unsqueeze(1).broadcast_to([128, 8, HD]).rearrange(
                "p h (u c) -> p h u c", u=2, c=32)
            v.tensor_tensor(a["rot2"][sb].rearrange(
                "p (h u c) -> p h u c", h=8, u=2, c=32), qv, ct, ALU.mult)
            v.tensor_tensor(a["rot2"][sb], a["rot2"][sb], a["rotmp"][sb],
                            ALU.add)
            for h in range(8):
                ins = v.tensor_scalar(
                    a["krot"][sb][:, 64 * h:64 * h + 64],
                    a["rot2"][sb][:, 64 * h:64 * h + 64],
                    t.rsqt[sb][:, h:h + 1], None, ALU.mult)
            sem.inc(ins, "VE", 1)
        kb.state[f"qkf_free_{sb}"] = sem.snap()

        if not has_tp:
            kb.state[f"krot_free_{sb}"] = sem.snap()
            continue

        @block.tensor
        def _(tr, sb=sb, sc0=sc0, ve0=ve0, pstp_pre=pstp_pre):
            tr.wait_ge(kb.h["VE"], ve0 + 2)
            for pr in range(4):
                if pr >= 2:
                    tr.wait_ge(kb.h["SC"], sc0 + pr + 1)
                else:
                    kb.w(tr, pstp_pre, "SC")
                sem.inc(tr.transpose(t.ps_tp[:, pr % 2],
                                     a["krot"][sb][:, 128 * pr:128 * pr + 128],
                                     t.ident[:]), "PE", 1)
        kb.state[f"krot_free_{sb}"] = sem.snap()
        kb.state["pstp_free"] = sem.snap()
    kb.state["qkv_ps_free"] = sem.snap()
    kb.freeflow = False


def rwkv_arena(kb):
    ar = kb.ar
    ar.reset()
    a = {}
    a["rk"] = [ar.f32(OWN) for _ in range(2)]
    a["rv"] = [ar.f32(OWN) for _ in range(2)]
    a["rz"] = ar.f32(RC + 1)
    a["rm"] = ar.f32(RC + 1)
    a["rd"] = ar.f32(RC)
    a["rtau"] = ar.f32(RC)
    a["rtv"] = ar.f32(RC)
    a["rP"] = ar.f32(RC)
    a["rQ"] = ar.f32(RC)
    a["obuf"] = ar.bf16(RC)
    a["wkv"] = [ar.bf16(2 * 8 * 128, 2, 8, 128) for _ in range(2)]
    a["sm"] = ar.f32(64)
    a["ramp"] = ar.f32(OWN)
    return a


def _rwkv_scan_chunk(kb, t, a, g, ch, carp, carr, out_writer):
    block, sem = kb.block, kb
    kb.freeflow = "rwkv" in FREE
    lo = RC * ch
    rk = a["rk"][g % 2][:, lo:lo + RC]
    rv = a["rv"][g % 2][:, lo:lo + RC]
    rz, rm = a["rz"], a["rm"]
    rd, rtau, rtv, rP, rQ = a["rd"], a["rtau"], a["rtv"], a["rP"], a["rQ"]
    ramp = a["ramp"][:, 0:RC]
    snap0 = sem.snap()
    sc0 = sem.c["SC"]
    ve0 = sem.c["VE"]

    @block.vector
    def _(v):
        kb.w(v, snap0, "SC", "GW", "CC")
        v.memset(rz[:, 0:1], NEG)
        v.tensor_scalar(rz[:, 1:], ramp, t.wcol[:, g:g + 1], None, ALU.mult)
        v.tensor_tensor(rz[:, 1:], rz[:, 1:], rk, ALU.add)
        v.memset(rm[:, 0:1], NEG)
        v.tensor_tensor_scan(rm[:, 1:], rz[:, 1:], rz[:, 1:], NEG,
                             ALU.max, ALU.max)
        v.memset(rd[:, 0:1], 0.0)
        kb.sp(v)
        v.tensor_tensor(rd[:, 1:], rm[:, 0:RC - 1], rm[:, 1:RC], ALU.subtract)
        sem.inc(v.tensor_tensor(rtau[:], rz[:, 0:RC], rm[:, 0:RC],
                                ALU.subtract), "VE", 1)

    @block.vector
    def _(v):
        v.wait_ge(kb.h["SC"], sc0 + 1)
        v.memset(rtv[:, 0:1], 0.0)
        v.tensor_tensor(rtv[:, 1:], rtau[:, 1:], rv[:, 0:RC - 1], ALU.mult)
        v.tensor_tensor_scan(rP[:], rd[:], rtv[:], 0.0, ALU.mult, ALU.add)
        v.tensor_tensor_scan(rQ[:], rd[:], rtau[:], 0.0, ALU.mult, ALU.add)
        kb.sp(v)
        sm = a["sm"]
        v.tensor_tensor(sm[:, 0:1], rm[:, RC:RC + 1], rz[:, RC:RC + 1], ALU.max)
        v.tensor_tensor(sm[:, 3:4], rm[:, RC - 1:RC], sm[:, 0:1], ALU.subtract)
        sem.inc(v.tensor_tensor(sm[:, 4:5], rz[:, RC:RC + 1], sm[:, 0:1],
                                ALU.subtract), "VE", 1)

    @block.vector
    def _(v):
        v.wait_ge(kb.h["SC"], sc0 + 2)
        sm = a["sm"]
        v.tensor_tensor(sm[:, 1:2], sm[:, 3:4], rP[:, RC - 1:RC], ALU.mult)
        v.tensor_tensor(sm[:, 5:6], sm[:, 4:5], rv[:, RC - 1:RC], ALU.mult)
        v.tensor_tensor(sm[:, 1:2], sm[:, 1:2], sm[:, 5:6], ALU.add)
        v.tensor_tensor(sm[:, 2:3], sm[:, 3:4], rQ[:, RC - 1:RC], ALU.mult)
        v.tensor_tensor(sm[:, 2:3], sm[:, 2:3], sm[:, 4:5], ALU.add)
        v.tensor_scalar(rd[:], ramp, t.wcol[:, g:g + 1], None, ALU.mult)
        v.tensor_tensor(rd[:], rm[:, 0:RC], rd[:], ALU.subtract)
        v.tensor_scalar(rd[:], rd[:], t.wcol[:, g:g + 1], None, ALU.add)
        v.tensor_scalar(rz[:, 0:RC], ramp, t.wcol[:, g:g + 1], None, ALU.mult)
        v.tensor_scalar(rz[:, 0:RC], rz[:, 0:RC], -1.0, carp, ALU.mult, ALU.add)
        v.tensor_scalar(rm[:, 0:RC], rk, t.ucol[:, g:g + 1], None, ALU.add)
        v.tensor_tensor(rtv[:], rd[:], rm[:, 0:RC], ALU.max)
        v.tensor_tensor(rtv[:], rtv[:], rz[:, 0:RC], ALU.max)
        v.tensor_tensor(rd[:], rd[:], rtv[:], ALU.subtract)
        v.tensor_tensor(rz[:, 0:RC], rz[:, 0:RC], rtv[:], ALU.subtract)
        sem.inc(v.tensor_tensor(rm[:, 0:RC], rm[:, 0:RC], rtv[:],
                                ALU.subtract), "VE", 1)

    @block.vector
    def _(v):
        v.wait_ge(kb.h["SC"], sc0 + 3)
        v.tensor_tensor(rtau[:], rd[:], rP[:], ALU.mult)
        v.scalar_tensor_tensor(rtau[:], rz[:, 0:RC], carr, rtau[:],
                               ALU.mult, ALU.add)
        v.tensor_tensor(rP[:], rm[:, 0:RC], rv, ALU.mult)
        v.tensor_tensor(rtau[:], rtau[:], rP[:], ALU.add)
        v.tensor_tensor(rtv[:], rd[:], rQ[:], ALU.mult)
        v.tensor_tensor(rtv[:], rtv[:], rm[:, 0:RC], ALU.add)
        sem.inc(v.tensor_tensor(rtv[:], rtv[:], rz[:, 0:RC], ALU.add), "VE", 1)

    @block.vector
    def _(v):
        v.wait_ge(kb.h["SC"], sc0 + 4)
        kb.w(v, kb.sget("obuf_free"), "GW")
        sem.inc(v.tensor_tensor(a["obuf"], rtau[:], rtv[:], ALU.mult), "VE", 1)
    s9 = sem.snap()

    @block.gpsimd
    def _(g_):
        kb.w(g_, s9, "VE")
        sem.inc(g_.dma_start(
            out=kb.d.orw_d.ap()[g, :, lo:lo + RC], in_=a["obuf"]), "GW", 16)
    kb.state["obuf_free"] = sem.snap()

    @block.vector
    def _(v):
        v.wait_ge(kb.h["SC"], sc0 + 4)
        sm = a["sm"]
        v.tensor_scalar(sm[:, 5:6], t.wcol[:, g:g + 1], float(-RC), carp,
                        ALU.mult, ALU.add)
        v.tensor_scalar(sm[:, 6:7], t.wcol[:, g:g + 1], float(-(RC - 1)),
                        sm[:, 0:1], ALU.mult, ALU.add)
        v.tensor_tensor(sm[:, 7:8], sm[:, 5:6], sm[:, 6:7], ALU.max)
        v.tensor_tensor(sm[:, 5:6], sm[:, 5:6], sm[:, 7:8], ALU.subtract)
        sem.inc(v.tensor_tensor(sm[:, 6:7], sm[:, 6:7], sm[:, 7:8],
                                ALU.subtract), "VE", 1)

    @block.vector
    def _(v):
        v.wait_ge(kb.h["SC"], sc0 + 5)
        sm = a["sm"]
        v.tensor_tensor(sm[:, 8:9], sm[:, 6:7], sm[:, 2:3], ALU.mult)
        v.tensor_tensor(sm[:, 8:9], sm[:, 8:9], sm[:, 5:6], ALU.add)
        v.tensor_tensor(sm[:, 9:10], sm[:, 6:7], sm[:, 1:2], ALU.mult)
        v.scalar_tensor_tensor(sm[:, 9:10], sm[:, 5:6], carr, sm[:, 9:10],
                               ALU.mult, ALU.add)
        sem.inc(v.engine_nop(), "VE", 1)

    @block.vector
    def _(v):
        v.wait_ge(kb.h["SC"], sc0 + 6)
        sm = a["sm"]
        v.tensor_tensor(carr, sm[:, 9:10], sm[:, 11:12], ALU.mult)
        v.tensor_tensor(carp, sm[:, 7:8], sm[:, 10:11], ALU.add)
        sem.inc(v.engine_nop(), "VE", 1)

    _ff = kb.freeflow
    kb.freeflow = True

    @block.scalar
    def _(sc):
        sc.wait_ge(kb.h["VE"], ve0 + 1)
        sc.activation(rd[:, 1:], rd[:, 1:], AF.Exp)
        sem.inc(sc.activation(rtau[:], rtau[:], AF.Exp), "SC", 1)
        sc.wait_ge(kb.h["VE"], ve0 + 2)
        sc.activation(a["sm"][:, 3:4], a["sm"][:, 3:4], AF.Exp)
        sem.inc(sc.activation(a["sm"][:, 4:5], a["sm"][:, 4:5], AF.Exp),
                "SC", 1)
        sc.wait_ge(kb.h["VE"], ve0 + 3)
        sc.activation(rd[:], rd[:], AF.Exp)
        sc.activation(rz[:, 0:RC], rz[:, 0:RC], AF.Exp)
        sem.inc(sc.activation(rm[:, 0:RC], rm[:, 0:RC], AF.Exp), "SC", 1)
        sc.wait_ge(kb.h["VE"], ve0 + 4)
        sc.activation(rtv[:], rtv[:], AF.Ln)
        sem.inc(sc.activation(rtv[:], rtv[:], AF.Exp, scale=-1.0), "SC", 1)
        sc.wait_ge(kb.h["VE"], ve0 + 6)
        sc.activation(a["sm"][:, 5:6], a["sm"][:, 5:6], AF.Exp)
        sem.inc(sc.activation(a["sm"][:, 6:7], a["sm"][:, 6:7], AF.Exp),
                "SC", 1)
        sc.wait_ge(kb.h["VE"], ve0 + 7)
        sc.activation(a["sm"][:, 10:11], a["sm"][:, 8:9], AF.Ln)
        sem.inc(sc.activation(a["sm"][:, 11:12], a["sm"][:, 10:11], AF.Exp,
                              scale=-1.0), "SC", 1)
    kb.freeflow = _ff
    kb.state["rwkv_carry"] = sem.snap()
    if ch == 1:
        kb.state[f"rwkv_kvfree_{g % 2}"] = sem.snap()
    kb.freeflow = False


def _rwkv_kv_mm(kb, t, a, g, tag):
    block, sem = kb.block, kb
    kb.freeflow = "rwkv" in FREE
    wsb = a["wkv"][g % 2]

    @block.gpsimd
    def _(gp):
        kb.w(gp, kb.sget(f"{tag}_wfree_{g % 2}"), "PE")
        sem.inc(gp.dma_start(out=wsb[:, 0], in_=kb.d.Wk_i[
            :, 128 * g:128 * g + 128].rearrange("(c p) n -> p c n", p=128)),
            "GW", 16)
        sem.inc(gp.dma_start(out=wsb[:, 1], in_=kb.d.Wv_i[
            :, 128 * g:128 * g + 128].rearrange("(c p) n -> p c n", p=128)),
            "GW", 16)
    snapw = sem.snap()

    @block.tensor
    def _(tr):
        kb.w(tr, snapw, "GW")
        kb.w(tr, kb.sget(f"rwkv_kvfree_{g % 2}"), "SC", "VE")
        kb.w(tr, kb.sget("rwkv_entry"), "SC", "VE", "GP", "GW")
        ev = {}
        for kv in range(2):
            for sec in range(4):
                i = kv * 4 + sec
                if i >= 2:
                    tr.wait_ge(kb.h["SC"], ev[i - 2])
                for c in range(8):
                    sem.inc(tr.matmul(
                        t.ps_a[:, i % 2], wsb[:, kv, c],
                        t.xn_own[:, c, 1 + 512 * sec:513 + 512 * sec],
                        start=(c == 0), stop=(c == 7)), "PE", 1)
                pe_at = sem.c["PE"]

                @block.scalar
                def _(sc, kv=kv, sec=sec, i=i, pe_at=pe_at):
                    sc.wait_ge(kb.h["PE"], pe_at)
                    dst = (a["rk"][g % 2] if kv == 0 else a["rv"][g % 2])
                    col = (t.bkcol if kv == 0 else t.bvcol)
                    sem.inc(sc.activation(dst[:, 512 * sec:512 * sec + 512],
                                          t.ps_a[:, i % 2], AF.Identity,
                                          bias=col[:, g:g + 1]), "SC", 1)
                ev[i] = sem.c["SC"]
    kb.state[f"{tag}_wfree_{g % 2}"] = sem.snap()
    kb.state[f"{tag}_mm_{g}"] = sem.snap()


def emit_rwkv_phaseA(kb, t, d):
    block, sem = kb.block, kb
    a = rwkv_arena(kb)
    kb.state["rwkvA_arena"] = a

    @block.gpsimd
    def _(g):
        kb.w(g, kb.sget("arena_free"), "PE", "VE", "SC", "GW")
        sem.inc(g.iota(a["ramp"], pattern=[[1, OWN]], base=0,
                       channel_multiplier=0,
                       allow_small_or_imprecise_dtypes=True), "GP", 1)
    kb.state["arena_free"] = sem.snap()

    rsub = int(os.environ.get("K_RSUB", "9"))
    ngr = int(os.environ.get("K_NGR", "8"))
    for g in range(ngr):
        _rwkv_kv_mm(kb, t, a, g, "rA")
        if rsub < 2:
            kb.state["rA_kvfree"] = kb.snap()
            continue
        snap = kb.state[f"rA_mm_{g}"]

        @block.vector
        def _(v, g=g, snap=snap):
            kb.freeflow = "rwkv" in FREE
            kb.w(v, snap, "SC")
            kb.w(v, kb.sget("arena_free"), "GP")
            kb.w(v, kb.sget("rA_sumfree"), "SC")
            for hh in range(2):
                zv = (a["rd"] if hh == 0 else a["rtau"])
                v.tensor_scalar(zv[:], a["ramp"][:, RC * hh:RC * hh + RC],
                                t.wcol[:, g:g + 1], None, ALU.mult)
                v.tensor_tensor(zv[:], zv[:],
                                a["rk"][g % 2][:, RC * hh:RC * hh + RC],
                                ALU.add)
            v.reduce_max(t.summ[:, 3 * g:3 * g + 1], a["rd"], AX.X, negate=True)
            v.reduce_max(a["sm"][:, 0:1], a["rtau"], AX.X, negate=True)
            kb.sp(v)
            v.tensor_tensor(t.summ[:, 3 * g:3 * g + 1],
                            t.summ[:, 3 * g:3 * g + 1], a["sm"][:, 0:1],
                            ALU.min)
            sem.inc(v.engine_nop(), "VE", 1)
        sz = sem.snap()

        @block.scalar
        def _(sc, g=g, sz=sz):
            kb.w(sc, sz, "VE")
            sc.activation(a["rd"][:], a["rd"][:], AF.Exp,
                          bias=t.summ[:, 3 * g:3 * g + 1])
            sem.inc(sc.activation(a["rtau"][:], a["rtau"][:], AF.Exp,
                                  bias=t.summ[:, 3 * g:3 * g + 1]), "SC", 1)
        se = sem.snap()

        @block.vector
        def _(v, g=g, se=se):
            kb.w(v, se, "SC")
            v.tensor_tensor(a["rtv"][:], a["rd"][:], a["rv"][g % 2][:, 0:RC],
                            ALU.mult)
            v.tensor_tensor(a["rz"][:, 0:RC], a["rtau"][:],
                            a["rv"][g % 2][:, RC:OWN], ALU.mult)
            v.tensor_tensor(a["rtv"][:], a["rtv"][:], a["rz"][:, 0:RC], ALU.add)
            v.reduce_sum(t.summ[:, 3 * g + 1:3 * g + 2], a["rtv"], AX.X)
            v.tensor_tensor(a["rtv"][:], a["rd"][:], a["rtau"][:], ALU.add)
            v.reduce_sum(t.summ[:, 3 * g + 2:3 * g + 3], a["rtv"], AX.X)
            sem.inc(v.tensor_scalar(t.summ[:, 3 * g:3 * g + 1],
                                    t.summ[:, 3 * g:3 * g + 1], -1.0, None,
                                    ALU.mult), "VE", 1)
        kb.state["rA_sumfree"] = sem.snap()
        kb.state[f"rwkv_kvfree_{g % 2}"] = sem.snap()
        kb.freeflow = False
    if rsub < 3:
        kb.state["ag1"] = kb.snap()
        kb.state["arena_free"] = kb.snap()
        return
    snap = sem.snap()

    @block.sync
    def _(s):
        kb.w(s, snap, "VE")
        sem.inc(s.dma_start(out=d.cc1_in.ap(), in_=t.summ[:]), "LD", 16)
    snap2 = sem.snap()

    @block.gpsimd
    def _(gp):
        kb.w(gp, snap2, "LD")
        sem.inc(gp.collective_compute(
            "AllGather", ALU.bypass, replica_groups=[list(range(8))],
            ins=[d.cc1_in.ap()], outs=[d.cc1_out.ap()]), "CC", 1)
        cc_at = sem.c["CC"]
        gp.wait_ge(kb.h["CC"], cc_at)
        sem.inc(gp.dma_start(out=t.gath1[:], in_=d.cc1_out.ap().transpose([1, 0, 2])), "GW", 16)
    kb.state["ag1"] = sem.snap()


def emit_carry_combine(kb, t):
    block, sem = kb.block, kb
    snap = kb.state["ag1"]
    cs = t.cscr

    @block.vector
    def _(v):
        kb.w(v, snap, "GW", "LD")
        for c in range(8):
            v.tensor_scalar(cs[:, 0:8], t.wcol[:], t.alpha_sb[:, c:c + 1],
                            t.maska_sb[:, c:c + 1], ALU.mult, ALU.subtract)
            v.tensor_tensor(cs[:, 8:16], t.gath1[:, c, :].rearrange(
                "p (g k) -> p g k", g=8, k=3)[:, :, 0], cs[:, 0:8],
                ALU.subtract)
            if c == 0:
                v.tensor_copy(t.carryp[:], cs[:, 8:16])
            else:
                v.tensor_tensor(t.carryp[:], t.carryp[:], cs[:, 8:16], ALU.max)
        v.memset(cs[:, 16:24], 0.0)
        sem.inc(v.memset(cs[:, 24:32], 0.0), "VE", 1)
    for c in range(8):
        s0 = sem.snap()

        @block.vector
        def _(v, c=c, s0=s0):
            kb.w(v, s0, "SC" if c else "VE")
            v.tensor_scalar(cs[:, 0:8], t.wcol[:], t.alpha_sb[:, c:c + 1],
                            t.maska_sb[:, c:c + 1], ALU.mult, ALU.subtract)
            v.tensor_tensor(cs[:, 8:16], t.gath1[:, c, :].rearrange(
                "p (g k) -> p g k", g=8, k=3)[:, :, 0], cs[:, 0:8],
                ALU.subtract)
            sem.inc(v.tensor_tensor(cs[:, 8:16], cs[:, 8:16], t.carryp[:],
                                    ALU.subtract), "VE", 1)
        ve_at = sem.c["VE"]

        @block.scalar
        def _(sc, ve_at=ve_at):
            sc.wait_ge(kb.h["VE"], ve_at)
            sem.inc(sc.activation(cs[:, 32:40], cs[:, 8:16], AF.Exp), "SC", 1)
        sc_at = sem.c["SC"]

        @block.vector
        def _(v, c=c, sc_at=sc_at):
            v.wait_ge(kb.h["SC"], sc_at)
            v.tensor_tensor(cs[:, 40:48], t.gath1[:, c, :].rearrange(
                "p (g k) -> p g k", g=8, k=3)[:, :, 1], cs[:, 32:40], ALU.mult)
            v.tensor_tensor(cs[:, 16:24], cs[:, 16:24], cs[:, 40:48], ALU.add)
            v.tensor_tensor(cs[:, 40:48], t.gath1[:, c, :].rearrange(
                "p (g k) -> p g k", g=8, k=3)[:, :, 2], cs[:, 32:40], ALU.mult)
            sem.inc(v.tensor_tensor(cs[:, 24:32], cs[:, 24:32], cs[:, 40:48],
                                    ALU.add), "VE", 1)
    s1 = sem.snap()

    @block.scalar
    def _(sc):
        kb.w(sc, s1, "VE")
        sc.activation(cs[:, 32:40], cs[:, 24:32], AF.Ln, bias=1e-38)
        sem.inc(sc.activation(cs[:, 40:48], cs[:, 32:40], AF.Exp, scale=-1.0),
                "SC", 1)
    s2 = sem.snap()

    @block.vector
    def _(v):
        kb.w(v, s2, "SC")
        v.tensor_tensor(t.carryr[:], cs[:, 16:24], cs[:, 40:48], ALU.mult)
        sem.inc(v.tensor_tensor(t.carryp[:], t.carryp[:], cs[:, 32:40],
                                ALU.add), "VE", 1)
    kb.state["carry0"] = sem.snap()


def emit_rwkv_phaseB(kb, t, d):
    block, sem = kb.block, kb
    emit_carry_combine(kb, t)
    a = kb.state["rwkvA_arena"]
    for g in range(8):
        _rwkv_kv_mm(kb, t, a, g, "rB")
        mm = kb.state[f"rB_mm_{g}"]
        car = kb.state["carry0"]

        @block.vector
        def _(v, g=g, mm=mm, car=car):
            kb.w(v, car, "VE")
            kb.w(v, mm, "SC")
            v.tensor_copy(t.cgrp[:, 0:1], t.carryp[:, g:g + 1])
            sem.inc(v.tensor_copy(t.cgrp[:, 1:2], t.carryr[:, g:g + 1]),
                    "VE", 1)
        for ch in range(2):
            _rwkv_scan_chunk(kb, t, a, g, ch, t.cgrp[:, 0:1], t.cgrp[:, 1:2],
                             None)
        kb.state["rB_kvfree"] = sem.snap()
    kb.state["rwkvB_done"] = sem.snap()
    kb.state["arena_free"] = sem.snap()


def attn_arena(kb, li):
    ar = kb.ar
    ar.reset()
    a = {}
    span = M1SPAN if li == 0 else OWN
    a["attnT"] = ar.bf16(8 * 512, 8, 512)
    a["otmw"] = ar.bf16(4 * 512, 4, 512)
    a["ofm"] = ar.bf16(4 * span, 4, span)
    a["wo"] = ar.bf16(4 * 1024, 4, 1024)
    a["wobuf"] = [ar.bf16(512) for _ in range(2)]
    if li == 0:
        a["xnm1h"] = ar.bf16(8 * 1024, 8, 1024)
        xt = ar.f32(1024)
        a["xtile"] = [xt, xt]
        a["xsq"] = ar.f32(1024)
        a["xntm"] = ar.bf16(1024)
    else:
        a["xm2"] = [ar.bf16(512) for _ in range(2)]
    return a


def emit_attention(kb, t, d, li, a):
    block, sem = kb.block, kb
    nw = 6 if li == 0 else 4
    ktile0 = 1 if li == 0 else K2OFF // 128
    qdep = kb.state[f"qkv_done{li}"]
    for wi in range(nw):
        for h in range(8):
            _attn_wh(kb, t, li, wi, h, ktile0, qdep, a)
        so = sem.snap()

        @block.tensor
        def _(tr, wi=wi):
            kb.w(tr, so, "VE")
            ev = {}
            i = 0
            for r in range(4):
                for p in range(4):
                    if i >= 2:
                        tr.wait_ge(kb.h["SC"], ev[i - 2])
                    else:
                        kb.w(tr, kb.sget("pstp_free"), "SC")
                    sem.inc(tr.transpose(t.ps_tp[:, i % 2],
                                         a["otmw"][:, r, 128 * p:128 * p + 128],
                                         t.ident[:]), "PE", 1)
                    pe_at = sem.c["PE"]

                    @block.scalar
                    def _(sc, r=r, p=p, pe_at=pe_at, wi=wi, i=i):
                        sc.wait_ge(kb.h["PE"], pe_at)
                        sem.inc(sc.copy(
                            a["ofm"][:, p, 512 * wi + 128 * r:
                                     512 * wi + 128 * r + 128],
                            t.ps_tp[:, i % 2]), "SC", 1)
                    ev[i] = sem.c["SC"]
                    i += 1
        kb.state["pstp_free"] = sem.snap()
        kb.state["otmw_free"] = sem.snap()
    kb.state[f"attn_done{li}"] = sem.snap()


def _attn_wh(kb, t, li, wi, h, ktile0, qdep, a):
    block, sem = kb.block, kb
    kb.freeflow = "attn" in FREE
    ho = h // 2
    po = 64 * (h % 2)
    qb = 512 + 512 * wi
    blocks = []
    for m in range(4):
        blocks.append((512 * wi + 128 * m, 0, 128 * (m + 1), m))
    for m in range(4):
        blocks.append((512 + 512 * wi + 128 * m, 128 * m, 512 - 128 * m, m))

    @block.tensor
    def _(tr):
        kb.w(tr, qdep, "SC", "VE")
        kb.w(tr, kb.state["attn_entry"], "VE", "SC", "GW", "GP", "PE")
        for bi, (kcol, qo, qw, m) in enumerate(blocks):
            kb.w(tr, kb.sget(f"exp_done_{bi % 2}"), "SC")
            sem.inc(tr.matmul(t.ps_st[:, bi % 2, 0:qw],
                              t.kfm[:, ho, kcol + 0:kcol + 128][po:po + 64]
                              if False else t.kfm[po:po + 64, ho, kcol:kcol + 128],
                              t.qfm[po:po + 64, ho, qb - 512 + qo:
                                    qb - 512 + qo + qw],
                              start=True, stop=True), "PE", 1)
            pe_at = sem.c["PE"]
            ktile = ktile0 + kcol // 128

            @block.scalar
            def _(sc, bi=bi, qw=qw, pe_at=pe_at, ktile=ktile):
                sc.wait_ge(kb.h["PE"], pe_at)
                kb.w(sc, kb.sget("attnT_free"), "PE")
                sc.wait_ge(kb.h["VE"], kb.sget("mask_done").get("VE", 0))
                sem.inc(sc.activation(a["attnT"][:, bi, 0:qw],
                                      t.ps_st[:, bi % 2, 0:qw], AF.Exp,
                                      bias=t.mb_sb[:, ktile:ktile + 1],
                                      scale=QK_SCALE), "SC", 1)
            kb.state[f"exp_done_{bi % 2}"] = sem.snap()
    se = sem.snap()

    @block.vector
    def _(v):
        kb.w(v, se, "SC")
        for m in range(4):
            v.tensor_tensor(a["attnT"][:, m, 128 * m:128 * m + 128],
                            a["attnT"][:, m, 128 * m:128 * m + 128],
                            t.trige[:], ALU.mult)
        ins = None
        for m in range(4):
            ins = v.tensor_tensor(a["attnT"][:, 4 + m, 0:128],
                                  a["attnT"][:, 4 + m, 0:128],
                                  t.trile[:], ALU.mult)
        sem.inc(ins, "VE", 1)
    kb.state["mask_done"] = sem.snap()
    sm = sem.snap()

    @block.tensor
    def _(tr):
        kb.w(tr, sm, "VE")
        kb.w(tr, kb.sget(f"pso_free_{h % 2}"), "VE", "SC")
        kb.w(tr, kb.sget("otmw_free"), "SC")
        for r in range(4):
            use = [(bi, kcol, qo) for bi, (kcol, qo, qw, m) in enumerate(blocks)
                   if (bi < 4 and m >= r) or (bi >= 4 and m <= r)]
            for j, (bi, kcol, qo) in enumerate(use):
                sem.inc(tr.matmul(
                    t.ps_o[:, h % 2, 65 * r:65 * r + 65],
                    a["attnT"][:, bi, 128 * r - qo:128 * r - qo + 128],
                    t.vtm[:, kcol // 128, 65 * h:65 * h + 65],
                    start=(j == 0), stop=(j == len(use) - 1)), "PE", 1)
    sav = sem.snap()
    kb.state["attnT_free"] = sem.snap()

    @block.vector
    def _(v):
        kb.w(v, sav, "PE")
        kb.w(v, kb.sget("recipT_free"), "VE")
        kb.w(v, kb.sget("otmw_free"), "PE")
        dens = t.ps_o[:, h % 2, 0:260].rearrange(
            "p (r c) -> p r c", r=4, c=65)[:, :, 64]
        v.tensor_scalar(t.recipT[:], dens, 1.0, 1e-12, ALU.mult, ALU.add)
        v.reciprocal(t.recipT[:], t.recipT[:])
        ins = None
        for r in range(4):
            ins = v.tensor_tensor(
                a["otmw"][:, r, 64 * h:64 * h + 64],
                t.ps_o[:, h % 2, 65 * r:65 * r + 64],
                t.recipT[:, r:r + 1].broadcast_to([128, 64]), ALU.mult)
        sem.inc(ins, "VE", 1)
    kb.state[f"pso_free_{h % 2}"] = sem.snap()
    kb.state["recipT_free"] = sem.snap()
    kb.freeflow = False


def emit_xnm1h(kb, t, a):
    for j in range(8):
        tt = 5 + j

        def wr(sc, c, ps, j=j):
            return sc.copy(a["xnm1h"][:, c, 128 * j:128 * j + 128], ps)
        emit_ln_tile(kb, t, a, tt, tt % 2, wr)
    kb.state["xnm1h_done"] = kb.snap()


def emit_wo(kb, t, d, li, a):
    block, sem = kb.block, kb
    nsec = 6 if li == 0 else 4
    if li == 0 and "o1" in d.taps:
        @block.gpsimd
        def _(g):
            kb.w(g, kb.state[f"attn_done{li}"], "SC", "VE", "PE")
            sem.inc(g.dma_start(out=d.taps["o1"], in_=a["ofm"]), "ST", 16)

    @block.gpsimd
    def _(g):
        kb.w(g, kb.sget(f"attn_done{li}"), "VE")
        sem.inc(g.dma_start(out=a["wo"], in_=d.wof_i[li].rearrange(
            "(c p) n -> p c n", p=128)), "GW", 16)
    snapw = sem.snap()
    adep = kb.state[f"attn_done{li}"]
    xdep = kb.state["xnm1h_done"] if li == 0 else kb.state["prologue"]
    for sec in range(nsec):
        for mc in range(8):
            pb = (sec * 8 + mc) % 2
            if li == 1:
                @block.gpsimd
                def _(g, sec=sec, mc=mc, pb=pb):
                    kb.w(g, kb.sget(f"xm2_free_{pb}"), "VE")
                    sem.inc(g.dma_start(
                        out=a["xm2"][pb],
                        in_=kb.d.x2_d.ap()[mc, :,
                                           1025 + 512 * sec:1537 + 512 * sec]),
                        "GW", 16)
            bdep = sem.snap()

            @block.tensor
            def _(tr, sec=sec, mc=mc, pb=pb):
                kb.w(tr, snapw, "GW")
                kb.w(tr, adep, "SC")
                kb.w(tr, kb.sget(f"wops_free_{pb}"), "VE")
                for c in range(4):
                    sem.inc(tr.matmul(
                        t.ps_m[:, pb],
                        a["wo"][:, c, 128 * mc:128 * mc + 128],
                        a["ofm"][:, c, 512 * sec:512 * sec + 512],
                        start=(c == 0), stop=(c == 3)), "PE", 1)
            pe_at = sem.c["PE"]

            @block.vector
            def _(v, sec=sec, mc=mc, pe_at=pe_at, li=li, pb=pb, bdep=bdep):
                v.wait_ge(kb.h["PE"], pe_at)
                kb.w(v, xdep, "SC")
                kb.w(v, bdep, "GW")
                kb.w(v, kb.sget(f"wobuf_free_{pb}"), "GW")
                if li == 0:
                    if sec < 2:
                        base = a["xnm1h"][:, mc, 512 * sec:512 * sec + 512]
                    else:
                        base = t.xn_own[:, mc, 1 + 512 * (sec - 2):
                                        513 + 512 * (sec - 2)]
                else:
                    base = a["xm2"][pb]
                sem.inc(v.tensor_tensor(a["wobuf"][pb], t.ps_m[:, pb], base,
                                        ALU.add), "VE", 1)
            kb.state[f"wops_free_{pb}"] = sem.snap()
            kb.state[f"xm2_free_{pb}"] = sem.snap()
            ve_at = sem.c["VE"]

            @block.gpsimd
            def _(g, sec=sec, mc=mc, ve_at=ve_at, pb=pb):
                g.wait_ge(kb.h["VE"], ve_at)
                sem.inc(g.dma_start(
                    out=kb.d.xmid_d.ap()[mc, :, 1 + 512 * sec:513 + 512 * sec],
                    in_=a["wobuf"][pb]), "GW", 16)
            kb.state[f"wobuf_free_{pb}"] = sem.snap()
    kb.state[f"wo_done{li}"] = sem.snap()



def mlp_arena(kb):
    ar = kb.ar
    ar.reset()
    a = {}
    a["xmsec"] = ar.bf16(8 * 514, 8, 514)
    a["hglu"] = ar.bf16(NC1 * 512, NC1, 512)
    a["xn2"] = [ar.bf16(8 * 512, 8, 512)]
    a["t1s"] = [ar.f32(512)]
    a["gelu"] = a["t1s"][0]
    a["hsq"] = [ar.bf16(512) for _ in range(2)]
    a["rstdB"] = ar.f32(512)
    a["negB"] = ar.f32(512)
    a["nbc"] = ar.f32(512)
    a["wa"] = [ar.bf16(8 * 128, 8, 128) for _ in range(2)]
    a["wg"] = [ar.bf16(8 * 128, 8, 128) for _ in range(2)]
    a["w2h"] = [ar.bf16(11 * 128, 11, 128) for _ in range(2)]
    a["x2b"] = [ar.bf16(512) for _ in range(2)]
    a["x3b"] = ar.f32(512)
    a["srow"] = ar.f32(1536)
    return a


def emit_mlp(kb, t, d, li):
    block, sem = kb.block, kb
    a = mlp_arena(kb)

    @block.vector
    def _(v):
        kb.w(v, kb.sget("arena_free"), "PE", "VE", "SC", "GW", "GP")
        sem.inc(v.engine_nop(), "VE", 1)
    kb.state["arena_free"] = sem.snap()

    @block.sync
    def _(s):
        sem.inc(s.dma_start(out=t.lng_sb[:], in_=d.lng_i[li]), "LD", 16)
        sem.inc(s.dma_start(out=t.lnb_sb[:], in_=d.lnb_i[li]), "LD", 16)
        sem.inc(s.dma_start(out=t.b1a_sb[:], in_=d.b1a_i[li]), "LD", 16)
        sem.inc(s.dma_start(out=t.b1g_sb[:], in_=d.b1g_i[li]), "LD", 16)
        sem.inc(s.dma_start(out=t.b2_sb[:], in_=d.b2_i[li]), "LD", 16)
    kb.state["mlp_params"] = sem.snap()
    for sec in range(6 if li == 0 else 4):
        _mlp_sec(kb, t, d, li, sec, a)
    kb.state[f"mlp_done{li}"] = sem.snap()
    kb.state["arena_free"] = sem.snap()


def _mlp_sec(kb, t, d, li, sec, a):
    block, sem = kb.block, kb
    wodep = kb.state[f"wo_done{li}"]

    @block.gpsimd
    def _(g):
        kb.w(g, wodep, "GW", "VE")
        kb.w(g, kb.sget("xmsec_free"), "PE", "VE", "SC")
        sem.inc(g.dma_start(out=a["xmsec"],
                            in_=kb.d.xmid_d.ap().transpose([1, 0, 2])
                            [:, :, 512 * sec:512 * sec + 514]), "GW", 16)
    sld = sem.snap()

    def hch(c, w=512):
        off = 0 if c >= 4 else 1
        return a["xmsec"][:, c, off:off + w]

    @block.tensor
    def _(tr):
        kb.w(tr, sld, "GW")
        kb.w(tr, kb.state["mlp_params"], "LD")
        kb.w(tr, kb.sget("psr_free"), "VE")
        kb.w(tr, kb.sget(f"x2ps_free_0"), "VE")
        kb.w(tr, kb.sget(f"x2ps_free_1"), "VE")
        for c in range(8):
            sem.inc(tr.matmul(t.ps_r[0], t.onescol[:], hch(c),
                              start=(c == 0), stop=(c == 7)), "PE", 1)
    pe_sq0 = sem.c["PE"]
    sc_sq0 = sem.c["SC"]
    hsq_pre = [kb.sget("hsq_free_0"), kb.sget("hsq_free_1")]

    @block.scalar
    def _(sc):
        kb.w(sc, sld, "GW")
        for c in range(8):
            if c < 2:
                kb.w(sc, hsq_pre[c % 2], "PE")
            else:
                sc.wait_ge(kb.h["PE"], pe_sq0 + c - 1)
            sem.inc(sc.activation(a["hsq"][c % 2], hch(c), AF.Square), "SC", 1)

    _ff = kb.freeflow
    kb.freeflow = True

    @block.tensor
    def _(tr):
        kb.w(tr, kb.sget("psr_free"), "VE")
        kb.w(tr, sld, "GW")
        for c in range(8):
            tr.wait_ge(kb.h["SC"], sc_sq0 + c + 1)
            sem.inc(tr.matmul(t.ps_r[1], t.onescol[:], a["hsq"][c % 2],
                              start=(c == 0), stop=(c == 7)), "PE", 1)
    kb.freeflow = _ff
    kb.state["hsq_free_0"] = sem.snap()
    kb.state["hsq_free_1"] = sem.snap()
    s2 = sem.snap()

    @block.vector
    def _(v):
        kb.w(v, s2, "PE")
        v.tensor_scalar(a["srow"][0:1, 0:512], t.ps_r[0], 1.0 / D, None, ALU.mult)
        v.tensor_scalar(a["srow"][0:1, 512:1024], t.ps_r[1], 1.0 / D, None,
                        ALU.mult)
        v.tensor_tensor(a["srow"][0:1, 1024:1536], a["srow"][0:1, 0:512],
                        a["srow"][0:1, 0:512], ALU.mult)
        sem.inc(v.tensor_tensor(a["srow"][0:1, 512:1024], a["srow"][0:1, 512:1024],
                                a["srow"][0:1, 1024:1536], ALU.subtract), "VE", 1)
    kb.state["psr_free"] = sem.snap()
    s3 = sem.snap()

    scL0 = sem.c["SC"]
    veL0 = sem.c["VE"]
    peL0 = sem.c["PE"]
    bc_pre = kb.sget("bc_free")

    @block.scalar
    def _(sc):
        kb.w(sc, s3, "VE")
        sc.activation(a["srow"][0:1, 512:1024], a["srow"][0:1, 512:1024], AF.Ln,
                      bias=1e-5)
        sem.inc(sc.activation(a["srow"][0:1, 512:1024], a["srow"][0:1, 512:1024],
                              AF.Exp, scale=-0.5), "SC", 1)
        sc.wait_ge(kb.h["PE"], peL0 + 2)
        sc.copy(a["rstdB"], t.ps_m[:, 0])
        sem.inc(sc.copy(a["negB"], t.ps_m[:, 1]), "SC", 1)

    _ff = kb.freeflow
    kb.freeflow = True

    @block.vector
    def _(v):
        v.wait_ge(kb.h["SC"], scL0 + 1)
        v.tensor_tensor(a["srow"][0:1, 1024:1536], a["srow"][0:1, 0:512],
                        a["srow"][0:1, 512:1024], ALU.mult)
        sem.inc(v.tensor_scalar(a["srow"][0:1, 1024:1536], a["srow"][0:1, 1024:1536],
                                -1.0, None, ALU.mult), "VE", 1)

    @block.tensor
    def _(tr):
        tr.wait_ge(kb.h["VE"], veL0 + 1)
        kb.w(tr, bc_pre, "SC")
        sem.inc(tr.matmul(t.ps_m[:, 0], t.onescolf[:],
                          a["srow"][0:1, 512:1024], start=True, stop=True),
                "PE", 1)
        sem.inc(tr.matmul(t.ps_m[:, 1], t.onescolf[:],
                          a["srow"][0:1, 1024:1536], start=True, stop=True),
                "PE", 1)
    kb.freeflow = _ff
    s7 = sem.snap()
    kb.state["bc_free"] = sem.snap()

    for c in range(8):
        @block.vector
        def _(v, c=c):
            kb.w(v, s7, "SC")
            kb.w(v, kb.sget("xn2_free"), "PE")
            v.tensor_tensor(a["t1s"][0], hch(c), a["rstdB"], ALU.mult)
            v.tensor_scalar(a["nbc"], a["negB"], t.lng_sb[:, c:c + 1],
                            t.lnb_sb[:, c:c + 1], ALU.mult, ALU.add)
            sem.inc(v.scalar_tensor_tensor(
                a["xn2"][0][:, c], a["t1s"][0],
                t.lng_sb[:, c:c + 1], a["nbc"], ALU.mult, ALU.add), "VE", 1)
    sn = sem.snap()

    kb.freeflow = "mlp" in FREE
    pe_w10 = sem.c["PE"]
    sc_w10 = sem.c["SC"]
    ve_w10 = sem.c["VE"]
    gelu_pre = kb.sget("gelu_free")
    glua_pre = [kb.sget("glua_free_0"), kb.sget("glua_free_1")]
    glug_pre = [kb.sget("glug_free_0"), kb.sget("glug_free_1")]
    for c in range(NC1):
        cw = 128 if c < NC1 - 1 else LASTC

        @block.gpsimd
        def _(g, c=c, cw=cw):
            kb.w(g, kb.sget(f"w1_free_{c % 2}"), "PE")
            sem.inc(g.dma_start(out=a["wa"][c % 2][:, :, 0:cw],
                                in_=d.w1_i[li, :, 128 * c:128 * c + cw]
                                .rearrange("(k p) n -> p k n", p=128)),
                    "GW", 16)
            sem.inc(g.dma_start(out=a["wg"][c % 2][:, :, 0:cw],
                                in_=d.w1_i[li, :, HID + 128 * c:HID + 128 * c + cw]
                                .rearrange("(k p) n -> p k n", p=128)),
                    "GW", 16)
        gw_at = sem.c["GW"]

        @block.tensor
        def _(tr, c=c, cw=cw, gw_at=gw_at):
            tr.wait_ge(kb.h["GW"], gw_at)
            kb.w(tr, sn, "VE")
            if c < 2:
                kb.w(tr, glua_pre[c % 2], "VE")
                kb.w(tr, glug_pre[c % 2], "SC")
            else:
                tr.wait_ge(kb.h["VE"], ve_w10 + c - 1)
                tr.wait_ge(kb.h["SC"], sc_w10 + c - 1)
            for k in range(8):
                sem.inc(tr.matmul(t.ps_m[0:cw, c % 2],
                                  a["wa"][c % 2][:, k, 0:cw],
                                  a["xn2"][0][:, k], start=(k == 0),
                                  stop=(k == 7)), "PE", 1)
            for k in range(8):
                sem.inc(tr.matmul(t.ps_g[0:cw, c % 2],
                                  a["wg"][c % 2][:, k, 0:cw],
                                  a["xn2"][0][:, k], start=(k == 0),
                                  stop=(k == 7)), "PE", 1)
        kb.state[f"w1_free_{c % 2}"] = sem.snap()

    _ff = kb.freeflow
    kb.freeflow = True

    @block.scalar
    def _(sc):
        for c in range(NC1):
            cw = 128 if c < NC1 - 1 else LASTC
            sc.wait_ge(kb.h["PE"], pe_w10 + 16 * (c + 1))
            if c == 0:
                kb.w(sc, gelu_pre, "VE")
            else:
                sc.wait_ge(kb.h["VE"], ve_w10 + c)
            sem.inc(sc.activation(a["gelu"][0:cw, :], t.ps_g[0:cw, c % 2],
                                  AF.Gelu, bias=t.b1g_sb[0:cw, c:c + 1]),
                    "SC", 1)

    @block.vector
    def _(v):
        for c in range(NC1):
            cw = 128 if c < NC1 - 1 else LASTC
            v.wait_ge(kb.h["SC"], sc_w10 + c + 1)
            sem.inc(v.scalar_tensor_tensor(
                a["hglu"][0:cw, c, :], t.ps_m[0:cw, c % 2],
                t.b1a_sb[0:cw, c:c + 1], a["gelu"][0:cw, :],
                ALU.add, ALU.mult), "VE", 1)
    kb.freeflow = _ff
    kb.state["glug_free_0"] = sem.snap()
    kb.state["glug_free_1"] = sem.snap()
    kb.state["glua_free_0"] = sem.snap()
    kb.state["glua_free_1"] = sem.snap()
    kb.state["gelu_free"] = sem.snap()
    sg = sem.snap()
    kb.state["xn2_free"] = sem.snap()

    for mc in range(8):
        pb = mc % 2
        for hh in range(2):
            @block.gpsimd
            def _(g, mc=mc, hh=hh):
                kb.w(g, kb.sget(f"w2h_free_{hh}"), "PE")
                k0 = 11 * hh
                if hh == 0:
                    sem.inc(g.dma_start(
                        out=a["w2h"][0],
                        in_=d.w2_i[li][0:1408, 128 * mc:128 * mc + 128]
                        .rearrange("(k p) n -> p k n", p=128)), "GW", 16)
                else:
                    sem.inc(g.dma_start(
                        out=a["w2h"][1][:, 0:10, :],
                        in_=d.w2_i[li][1408:2688, 128 * mc:128 * mc + 128]
                        .rearrange("(k p) n -> p k n", p=128)), "GW", 16)
                    sem.inc(g.dma_start(
                        out=a["w2h"][1][0:LASTC, 10, :],
                        in_=d.w2_i[li][2688:HID, 128 * mc:128 * mc + 128]),
                        "GW", 16)
            gw_at = sem.c["GW"]

            @block.tensor
            def _(tr, mc=mc, hh=hh, gw_at=gw_at, pb=pb):
                tr.wait_ge(kb.h["GW"], gw_at)
                kb.w(tr, sg, "VE")
                if hh == 0:
                    kb.w(tr, kb.sget(f"x2ps_free_{pb}"), "VE")
                for k in range(11):
                    kc = 11 * hh + k
                    kw = 128 if kc < NC1 - 1 else LASTC
                    sem.inc(tr.matmul(t.ps_m[:, pb],
                                      a["w2h"][hh][0:kw, k, :],
                                      a["hglu"][0:kw, kc, :],
                                      start=(kc == 0), stop=(kc == NC1 - 1)),
                            "PE", 1)
            kb.state[f"w2h_free_{hh}"] = sem.snap()
        pe_at = sem.c["PE"]
        if li == 0:
            @block.vector
            def _(v, mc=mc, pe_at=pe_at, pb=pb):
                v.wait_ge(kb.h["PE"], pe_at)
                kb.w(v, kb.sget(f"x2b_free_{pb}"), "GW")
                sem.inc(v.scalar_tensor_tensor(
                    a["x2b"][pb], t.ps_m[:, pb], t.b2_sb[:, mc:mc + 1],
                    a["xmsec"][:, mc, 1:513], ALU.add, ALU.add), "VE", 1)
            ve_at = sem.c["VE"]

            @block.gpsimd
            def _(g, mc=mc, ve_at=ve_at, pb=pb):
                g.wait_ge(kb.h["VE"], ve_at)
                sem.inc(g.dma_start(
                    out=kb.d.x2_d.ap()[mc, :, 1 + 512 * sec:513 + 512 * sec],
                    in_=a["x2b"][pb]), "GW", 16)
            kb.state[f"x2b_free_{pb}"] = sem.snap()
        else:
            @block.vector
            def _(v, mc=mc, pe_at=pe_at, pb=pb):
                v.wait_ge(kb.h["PE"], pe_at)
                kb.w(v, kb.sget("x3b_free"), "GW")
                sem.inc(v.scalar_tensor_tensor(
                    a["x3b"], t.ps_m[:, pb], t.b2_sb[:, mc:mc + 1],
                    a["xmsec"][:, mc, 1:513], ALU.add, ALU.add), "VE", 1)
            ve_at = sem.c["VE"]

            @block.gpsimd
            def _(g, mc=mc, ve_at=ve_at):
                g.wait_ge(kb.h["VE"], ve_at)
                sem.inc(g.dma_start(
                    out=kb.d.x3_d.ap()[mc, :, 512 * sec:512 * sec + 512],
                    in_=a["x3b"]), "GW", 16)
            kb.state["x3b_free"] = sem.snap()
        kb.state[f"x2ps_free_{pb}"] = sem.snap()
    kb.state["xmsec_free"] = sem.snap()
    kb.freeflow = False




def emit_ag2(kb, t, d):
    block, sem = kb.block, kb
    dep = kb.state["wo_done1"]

    @block.gpsimd
    def _(g):
        kb.w(g, dep, "GW")
        sem.inc(g.dma_start(out=d.cc2_in.ap(),
                            in_=d.xmid_d.ap().transpose([1, 0, 2])
                            [:, :, 2048:2049].squeeze(2),
                            allow_slow_non_contiguous=True), "GW", 16)
        gw_at = sem.c["GW"]
        g.wait_ge(kb.h["GW"], gw_at)
        sem.inc(g.collective_compute(
            "AllGather", ALU.bypass, replica_groups=[list(range(8))],
            ins=[d.cc2_in.ap()], outs=[d.cc2_out.ap()]), "CC", 1)
        g.wait_ge(kb.h["CC"], sem.c["CC"])
        sem.inc(g.dma_start(out=t.gath2[:],
                            in_=d.cc2_out.ap().transpose([1, 0, 2])), "GW", 16)
    s0 = sem.snap()

    @block.vector
    def _(v):
        kb.w(v, s0, "GW")
        v.memset(t.xm2a[0][:], 0.0)
        for c in range(8):
            v.scalar_tensor_tensor(t.xm2a[(c + 1) % 2][:], t.gath2[:, c, :],
                                   t.selx_sb[:, c:c + 1], t.xm2a[c % 2][:],
                                   ALU.mult, ALU.add)
        sem.inc(v.engine_nop(), "VE", 1)
    s1 = sem.snap()

    @block.gpsimd
    def _(g):
        kb.w(g, s1, "VE")
        sem.inc(g.dma_start(out=d.xmid_d.ap().transpose([1, 0, 2])
                            [:, :, 0:1].squeeze(2), in_=t.xm2a[0][:],
                            allow_slow_non_contiguous=True), "GW", 16)
    kb.state["ag2_done"] = sem.snap()


def final_arena(kb):
    ar = kb.ar
    ar.reset()
    a = {}
    a["x3t"] = [ar.f32(1024, 8, 128) for _ in range(2)]
    a["orwt"] = [ar.bf16(8 * 128, 8, 128) for _ in range(2)]
    a["xown"] = [ar.f32(1024) for _ in range(2)]
    a["gsil"] = ar.f32(1024)
    a["tmp"] = ar.f32(512)
    a["outt"] = [ar.f32(1024) for _ in range(2)]
    a["wor"] = ar.bf16(8 * 1024, 8, 1024)
    a["borep"] = ar.f32(1024)
    return a


def emit_final(kb, t, d):
    block, sem = kb.block, kb
    a = final_arena(kb)
    dep = kb.state["mlp_done1"]

    @block.gpsimd
    def _(g):
        kb.w(g, kb.sget("arena_free"), "PE", "VE", "SC", "GW", "GP")
        sem.inc(g.dma_start(out=a["wor"], in_=d.Wor_i.rearrange(
            "(c p) n -> p c n", p=128)), "GW", 16)
        sem.inc(g.dma_start(out=a["borep"], in_=d.borep_i), "GW", 16)
    kb.state["arena_free"] = sem.snap()
    worsnap = sem.snap()
    for tau in range(16):
        pb = tau % 2

        @block.gpsimd
        def _(g, tau=tau, pb=pb):
            kb.w(g, dep, "GW")
            kb.w(g, kb.sget(f"fin_free_{pb}"), "VE", "SC")
            sem.inc(g.dma_start(out=a["x3t"][pb],
                                in_=d.x3_d.ap().transpose([1, 0, 2])
                                [:, :, 128 * tau:128 * tau + 128]), "GW", 16)
            sem.inc(g.dma_start(out=a["orwt"][pb],
                                in_=d.orw_d.ap().transpose([1, 0, 2])
                                [:, :, 128 * tau:128 * tau + 128]), "GW", 16)

        @block.sync
        def _(s, tau=tau, pb=pb):
            kb.w(s, kb.sget(f"fin_free_{pb}"), "VE")
            sem.inc(s.dma_start(
                out=a["xown"][pb],
                in_=d.xs[Q2OFF + 128 * tau:Q2OFF + 128 * tau + 128, :]),
                "LD", 16)
        sld = sem.snap()

        gfree = kb.sget("gsil_free")

        pe0f = sem.c["PE"]
        sc0f = sem.c["SC"]
        tpf_pre = kb.sget("tpf_free")
        tpf2 = [t.ps_tpf, t.ps_tpf2]

        @block.tensor
        def _(tr, tau=tau, pb=pb):
            kb.w(tr, sld, "GW")
            kb.w(tr, gfree, "VE")
            for c in range(8):
                if c >= 2:
                    tr.wait_ge(kb.h["SC"], sc0f + c - 1)
                else:
                    kb.w(tr, tpf_pre, "SC")
                sem.inc(tr.transpose(tpf2[c % 2][:], a["x3t"][pb][:, c, :],
                                     t.identf[:]), "PE", 1)

        _ff = kb.freeflow
        kb.freeflow = True

        @block.scalar
        def _(sc, gfree=gfree, pe0f=pe0f):
            kb.w(sc, gfree, "VE")
            for c in range(8):
                sc.wait_ge(kb.h["PE"], pe0f + c + 1)
                sem.inc(sc.activation(a["gsil"][:, 128 * c:128 * c + 128],
                                      tpf2[c % 2][:], AF.Silu), "SC", 1)
        kb.freeflow = _ff
        kb.state["tpf_free"] = sem.snap()
        ssil = sem.snap()

        for hh in range(2):
            @block.tensor
            def _(tr, tau=tau, pb=pb, hh=hh):
                kb.w(tr, sld, "GW")
                kb.w(tr, worsnap, "GW")
                kb.w(tr, kb.sget(f"sps_free_{hh}"), "VE")
                for g in range(8):
                    sem.inc(tr.matmul(t.ps_a[:, hh], a["orwt"][pb][:, g, :],
                                      a["wor"][:, g, 512 * hh:512 * hh + 512],
                                      start=(g == 0), stop=(g == 7)), "PE", 1)
            pe_at = sem.c["PE"]

            @block.vector
            def _(v, tau=tau, pb=pb, hh=hh, pe_at=pe_at):
                v.wait_ge(kb.h["PE"], pe_at)
                kb.w(v, ssil, "SC")
                kb.w(v, sld, "LD")
                kb.w(v, kb.sget(f"outt_free_{pb}"), "ST")
                sl = slice(512 * hh, 512 * hh + 512)
                v.tensor_tensor(a["tmp"], t.ps_a[:, hh], a["borep"][:, sl],
                                ALU.add)
                v.tensor_tensor(a["tmp"], a["tmp"], a["gsil"][:, sl], ALU.mult)
                sem.inc(v.tensor_tensor(a["outt"][pb][:, sl], a["tmp"],
                                        a["xown"][pb][:, sl], ALU.add),
                        "VE", 1)
            kb.state[f"sps_free_{hh}"] = sem.snap()
        kb.state[f"fin_free_{pb}"] = sem.snap()
        kb.state["gsil_free"] = sem.snap()
        ve_at = sem.c["VE"]

        @block.sync
        def _(s, tau=tau, pb=pb, ve_at=ve_at):
            s.wait_ge(kb.h["VE"], ve_at)
            sem.inc(s.dma_start(out=d.y[128 * tau:128 * tau + 128, :],
                                in_=a["outt"][pb]), "ST", 16)
        kb.state[f"outt_free_{pb}"] = sem.snap()
    kb.state["final_done"] = sem.snap()


def build():
    nc = bass.Bass("TRN2", target_bir_lowering=False)
    dt = nc.dram_tensor

    class NS:
        pass
    d = NS()
    d.xs = dt("xs", [XSPAN, D], F32, kind="ExternalInput").ap()
    d.consts = dt("consts", [128, 4], F32, kind="ExternalInput").ap()
    d.maskbias = dt("maskbias", [128, NT], F32, kind="ExternalInput").ap()
    d.goff = dt("goff", [128, 1], F32, kind="ExternalInput").ap()
    d.alpha_in = dt("alpha_in", [128, 8], F32, kind="ExternalInput").ap()
    d.maskadd = dt("maskadd", [128, 8], F32, kind="ExternalInput").ap()
    d.selx = dt("selx", [128, 8], F32, kind="ExternalInput").ap()
    d.wcol_i = dt("wcol", [128, 8], F32, kind="ExternalInput").ap()
    d.ucol_i = dt("ucol", [128, 8], F32, kind="ExternalInput").ap()
    d.bkcol_i = dt("bkcol", [128, 8], F32, kind="ExternalInput").ap()
    d.bvcol_i = dt("bvcol", [128, 8], F32, kind="ExternalInput").ap()
    d.borep_i = dt("borep", [128, D], F32, kind="ExternalInput").ap()
    d.lng_i = dt("lng", [2, 128, 8], F32, kind="ExternalInput").ap()
    d.lnb_i = dt("lnb", [2, 128, 8], F32, kind="ExternalInput").ap()
    d.b1a_i = dt("b1a", [2, 128, NC1], F32, kind="ExternalInput").ap()
    d.b1g_i = dt("b1g", [2, 128, NC1], F32, kind="ExternalInput").ap()
    d.b2_i = dt("b2c", [2, 128, 8], F32, kind="ExternalInput").ap()
    d.qs_i = dt("qs_rep", [2, 128, HD], F32, kind="ExternalInput").ap()
    d.qsp_i = dt("qsp_rep", [2, 128, HD], F32, kind="ExternalInput").ap()
    d.ks_i = dt("ks_rep", [2, 128, HD], F32, kind="ExternalInput").ap()
    d.ksp_i = dt("ksp_rep", [2, 128, HD], F32, kind="ExternalInput").ap()
    d.ident_i = dt("ident", [128, 128], F32, kind="ExternalInput").ap()
    d.trile_i = dt("trile", [128, 128], F32, kind="ExternalInput").ap()
    d.trige_i = dt("trige", [128, 128], F32, kind="ExternalInput").ap()
    d.Wk_i = dt("Wk", [D, D], F32, kind="ExternalInput").ap()
    d.Wv_i = dt("Wv", [D, D], F32, kind="ExternalInput").ap()
    d.Wor_i = dt("Wor", [D, D], F32, kind="ExternalInput").ap()
    d.wqkv_i = dt("wqkv", [2, D, 1536], F32, kind="ExternalInput").ap()
    d.wof_i = dt("wof", [2, 512, D], F32, kind="ExternalInput").ap()
    d.w1_i = dt("w1", [2, D, 2 * HID], F32, kind="ExternalInput").ap()
    d.w2_i = dt("w2", [2, HID, D], F32, kind="ExternalInput").ap()
    d.y = dt("y", [OWN, D], F32, kind="ExternalOutput").ap()
    d.cc1_in = dt("cc1_in", [128, 24], F32)
    d.cc1_out = dt("cc1_out", [8, 128, 24], F32, addr_space="Shared")
    d.cc2_in = dt("cc2_in", [128, 8], F32)
    d.cc2_out = dt("cc2_out", [8, 128, 8], F32, addr_space="Shared")
    d.xmid_d = dt("xmid_d", [8, 128, M1SPAN + 2], BF16)
    d.x2_d = dt("x2_d", [8, 128, M1SPAN + 2], BF16)
    d.orw_d = dt("orw_d", [8, 128, OWN], BF16)
    d.x3_d = dt("x3_d", [8, 128, OWN], F32)
    d.taps = {}
    for name, shape in [("xn", [128, 8, OWN + 1]),
                        ("q1", [128, 4, K1SPAN - 512]),
                        ("k1", [128, 4, K1SPAN]), ("v1", [128, NT - 1, 520]),
                        ("summ", [128, 24]), ("carry", [128, 16]),
                        ("orwkv", [8, 128, OWN]), ("o1", [128, 4, M1SPAN]),
                        ("xmid1", [8, 128, M1SPAN + 2]),
                        ("x2", [8, 128, M1SPAN + 2]),
                        ("xmid2", [8, 128, M1SPAN + 2]),
                        ("x3", [8, 128, OWN])]:
        if name in TAPS:
            d.taps[name] = dt("tap_" + name, shape,
                              F32 if name not in
                              ("orwkv", "xmid1", "x2", "xmid2", "x3")
                              else (F32 if name == "x3" else F32),
                              kind="ExternalOutput").ap()

    ctx = contextlib.ExitStack()
    S = lambda n, sh, ty=F32: ctx.enter_context(nc.sbuf_tensor(n, sh, ty))
    PT = lambda n, sh, ty=F32: ctx.enter_context(nc.psum_tensor(n, sh, ty))
    with ctx:
        t = NS()
        t.arena = S("arena", [128, ARENA_N])
        t.xn_own = S("xn_own", [128, 8, OWN + 1], BF16)
        t.xn_roll = [S(f"xn_roll{i}", [128, 8, 129], BF16) for i in range(3)]
        t.qfm = S("qfm", [128, 4, K1SPAN - 512], BF16)
        t.kfm = S("kfm", [128, 4, K1SPAN], BF16)
        t.vtm = S("vtm", [128, NT - 1, 520], BF16)
        t.ident = S("ident_sb", [128, 128], BF16)
        t.identf = S("identf_sb", [128, 128], F32)
        t.trile = S("trile_sb", [128, 128], BF16)
        t.trige = S("trige_sb", [128, 128], BF16)
        t.onescol = S("onescol", [128, 1], BF16)
        t.onescolf = S("onescolf", [1, 128])
        t.zcol = S("zcol", [128, 8])
        t.consts = S("consts_sb", [128, 4])
        t.junk = S("junk", [128, 128])
        t.mb_sb = S("mb_sb", [128, NT])
        t.goff_sb = S("goff_sb", [128, 1])
        t.invfreq = S("invfreq", [128, HD])
        t.halfidx = S("halfidx", [128, HD])
        t.lnbase = S("lnbase", [128, HD])
        t.qs_sb = S("qs_sb", [128, HD])
        t.qsp_sb = S("qsp_sb", [128, HD])
        t.ks_sb = S("ks_sb", [128, HD])
        t.ksp_sb = S("ksp_sb", [128, HD])
        t.wcol = S("wcol_sb", [128, 8])
        t.ucol = S("ucol_sb", [128, 8])
        t.bkcol = S("bkcol_sb", [128, 8])
        t.bvcol = S("bvcol_sb", [128, 8])
        t.alpha_sb = S("alpha_sb", [128, 8])
        t.maska_sb = S("maska_sb", [128, 8])
        t.selx_sb = S("selx_sb", [128, 8])
        t.lng_sb = S("lng_sb", [128, 8])
        t.lnb_sb = S("lnb_sb", [128, 8])
        t.b1a_sb = S("b1a_sb", [128, NC1])
        t.b1g_sb = S("b1g_sb", [128, NC1])
        t.b2_sb = S("b2_sb", [128, 8])
        t.iota29 = S("iota29", [128, NT])
        t.tpos = S("tpos", [128, NT])
        t.tpw = S("tpw", [128, NT])
        t.stat = S("stat", [128, 6])
        t.rsqt = [S(f"rsqt{i}", [128, HEADS]) for i in range(2)]
        t.summ = S("summ", [128, 24])
        t.gath1 = S("gath1", [128, 8, 24])
        t.gath2 = S("gath2", [128, 8, 8])
        t.carryp = S("carryp", [128, 8])
        t.carryr = S("carryr", [128, 8])
        t.cgrp = S("cgrp", [128, 2])
        t.cscr = S("cscr", [128, 48])
        t.recipT = S("recipT", [128, 4])
        t.xm2col = S("xm2col", [128, 8])
        t.xm2a = [S(f"xm2a{i}", [128, 8]) for i in range(2)]
        t.ps_a = PT("ps_a", [128, 2, 512])
        t.ps_b = PT("ps_b", [128, 2, 512])
        t.ps_c = PT("ps_c", [128, 2, 512])
        t.ps_tp3 = PT("ps_tp3", [128, 2, 1024], BF16)
        t.ps_tp = t.ps_tp3[:, :, 0:128]
        t.ps_tpf = t.ps_tp3[:].bitcast(F32)[:, 0, 0:128]
        t.ps_tpf2 = t.ps_tp3[:].bitcast(F32)[:, 1, 0:128]
        t.ps_qkv = [t.ps_a[:, 0], t.ps_a[:, 1], t.ps_b[:, 0]]
        t.ps_st = t.ps_a
        t.ps_o = t.ps_b
        t.ps_m = t.ps_c
        t.ps_g = t.ps_b
        t.ps_r = [t.ps_c[0:1, 0, :], t.ps_c[0:1, 1, :]]

        for i, val in enumerate((1e-5, 1e-24, 1e-35, 1e-38)):
            nc.const_aps.register(F32, val, t.consts[:, i:i + 1])
        kb = KB(nc, ctx.enter_context(nc.Block()), ctx,
                ["LD", "GW", "PE", "VE", "SC", "GP", "CC", "ST"])
        kb.junk = t.junk
        kb.d = d
        kb.d_xs = d.xs
        kb.ar = Arena(t.arena)

        emit_prologue(kb, t, d)
        if STAGE >= 1:
            emit_qkv(kb, t, d, 0)
        if STAGE >= 2:
            kb.state["rwkv_entry"] = kb.snap()
            kb.state["rwkv_kvfree_0"] = {}
            kb.state["rwkv_kvfree_1"] = {}
            emit_rwkv_phaseA(kb, t, d)
        if STAGE >= 3:
            emit_rwkv_phaseB(kb, t, d)
        if STAGE >= 4:
            aa = attn_arena(kb, 0)
            kb.state["arena_free"] = kb.snap()
            kb.state["attn_entry"] = kb.snap()
            emit_attention(kb, t, d, 0, aa)
            emit_xnm1h(kb, t, aa)
            emit_wo(kb, t, d, 0, aa)
            kb.state["arena_free"] = kb.snap()
        if STAGE >= 5:
            emit_mlp(kb, t, d, 0)
        if STAGE >= 6:
            emit_qkv(kb, t, d, 1)
        if STAGE >= 7:
            aa = attn_arena(kb, 1)
            kb.state["arena_free"] = kb.snap()
            kb.state["attn_entry"] = kb.snap()
            emit_attention(kb, t, d, 1, aa)
            emit_wo(kb, t, d, 1, aa)
            kb.state["arena_free"] = kb.snap()
        if STAGE >= 8:
            emit_ag2(kb, t, d)
        if STAGE >= 9:
            emit_mlp(kb, t, d, 1)
        if STAGE >= 10:
            emit_final(kb, t, d)
        _finish(kb, t, d)
    return nc


def _finish(kb, t, d):
    block, sem = kb.block, kb
    fin = sem.snap()

    @block.sync
    def _(s):
        kb.w(s, fin, "PE", "VE", "SC", "GP", "CC", "GW", "LD")
        if STAGE < 10:
            sem.inc(s.dma_start(out=d.y[0:128, :], in_=t.arena[:, 0:1024]),
                    "ST", 16)
        s.wait_ge(kb.h["ST"], sem.c["ST"])

    @block.gpsimd
    def _(g):
        kb.w(g, fin, "PE", "VE", "SC", "CC", "LD")
        for name, tile in [("xn", t.xn_own[:]), ("q1", t.qfm[:]),
                           ("k1", t.kfm[:]), ("v1", t.vtm[:]),
                           ("summ", t.summ[:]),
                           ("orwkv", d.orw_d.ap())]:
            if name in d.taps:
                sem.inc(g.dma_start(out=d.taps[name], in_=tile), "ST", 16)
        if "carry" in d.taps:
            sem.inc(g.dma_start(out=d.taps["carry"][:, 0:8], in_=t.carryp[:]),
                    "ST", 16)
            sem.inc(g.dma_start(out=d.taps["carry"][:, 8:16], in_=t.carryr[:]),
                    "ST", 16)
        if "xmid1" in d.taps:
            sem.inc(g.dma_start(out=d.taps["xmid1"], in_=d.xmid_d.ap()),
                    "ST", 16)
        if "xmid2" in d.taps:
            sem.inc(g.dma_start(out=d.taps["xmid2"], in_=d.xmid_d.ap()),
                    "ST", 16)
        if "x2" in d.taps:
            sem.inc(g.dma_start(out=d.taps["x2"], in_=d.x2_d.ap()), "ST", 16)
        if "x3" in d.taps:
            sem.inc(g.dma_start(out=d.taps["x3"], in_=d.x3_d.ap()), "ST", 16)
        g.wait_ge(kb.h["ST"], sem.c["ST"])


def _prep(inp):
    x = np.ascontiguousarray(inp["L0_attn_x"], dtype=np.float32)
    sh = {}
    sh["wcol"] = np.ascontiguousarray(
        np.asarray(inp["L0_attn_w_rwkv"], np.float32).reshape(8, 128).T)
    sh["ucol"] = np.ascontiguousarray(
        np.asarray(inp["L0_attn_u_rwkv"], np.float32).reshape(8, 128).T)
    sh["bkcol"] = np.ascontiguousarray(
        np.asarray(inp["L0_attn_bk"], np.float32).reshape(8, 128).T)
    sh["bvcol"] = np.ascontiguousarray(
        np.asarray(inp["L0_attn_bv"], np.float32).reshape(8, 128).T)
    sh["borep"] = np.ascontiguousarray(np.broadcast_to(
        np.asarray(inp["L0_attn_bo"], np.float32), (128, D)))
    sh["lng"] = np.ascontiguousarray(
        np.asarray(inp["L0_ffn_ln_g"], np.float32).reshape(2, 8, 128)
        .transpose(0, 2, 1))
    sh["lnb"] = np.ascontiguousarray(
        np.asarray(inp["L0_ffn_ln_b"], np.float32).reshape(2, 8, 128)
        .transpose(0, 2, 1))
    b1 = np.asarray(inp["L0_ffn_b1"], np.float32)
    b1a = np.zeros((2, 128, NC1), np.float32)
    b1g = np.zeros((2, 128, NC1), np.float32)
    for li in range(2):
        pa = np.zeros(NC1 * 128, np.float32)
        pa[:HID] = b1[li, :HID]
        pg = np.zeros(NC1 * 128, np.float32)
        pg[:HID] = b1[li, HID:]
        b1a[li] = pa.reshape(NC1, 128).T
        b1g[li] = pg.reshape(NC1, 128).T
    sh["b1a"], sh["b1g"] = b1a, b1g
    sh["b2c"] = np.ascontiguousarray(
        np.asarray(inp["L0_ffn_b2"], np.float32).reshape(2, 8, 128)
        .transpose(0, 2, 1))
    partner = np.concatenate([np.arange(32, 64), np.arange(0, 32)])
    qs = np.asarray(inp["L0_ffn_q_scale"], np.float32)
    ks = np.asarray(inp["L0_ffn_k_scale"], np.float32)
    sh["qs_rep"] = np.ascontiguousarray(np.broadcast_to(qs[:, None, :], (2, 128, HD)))
    sh["qsp_rep"] = np.ascontiguousarray(
        np.broadcast_to(qs[:, partner][:, None, :], (2, 128, HD)))
    sh["ks_rep"] = np.ascontiguousarray(np.broadcast_to(ks[:, None, :], (2, 128, HD)))
    sh["ksp_rep"] = np.ascontiguousarray(
        np.broadcast_to(ks[:, partner][:, None, :], (2, 128, HD)))
    sh["ident"] = np.eye(128, dtype=np.float32)
    sh["trile"] = np.tril(np.ones((128, 128), np.float32)).T.copy()
    sh["trige"] = np.triu(np.ones((128, 128), np.float32)).T.copy()
    sh["Wk"] = np.asarray(inp["L0_attn_Wk"], np.float32)
    sh["Wv"] = np.asarray(inp["L0_attn_Wv"], np.float32)
    sh["Wor"] = np.asarray(inp["L0_attn_Wo"], np.float32)
    sh["wqkv"] = np.asarray(inp["L0_ffn_wqkv"], np.float32)
    sh["wof"] = np.asarray(inp["L0_ffn_wo"], np.float32)
    sh["w1"] = np.asarray(inp["L0_ffn_w1"], np.float32)
    sh["w2"] = np.asarray(inp["L0_ffn_w2"], np.float32)

    in_maps = []
    for core in range(8):
        b, qi = core // 4, core % 4
        t0 = qi * OWN
        lo = t0 - XH
        xs = np.zeros((XSPAN, D), np.float32)
        src_lo = max(lo, 0)
        xs[src_lo - lo:] = x[b, src_lo:t0 + OWN]
        npad = src_lo - lo
        mb = np.full(XSPAN, EXP_BIAS, np.float32)
        mb[:npad] = -1e9
        m = dict(sh)
        m["xs"] = xs
        m["consts"] = np.broadcast_to(
            np.array([1e-5, 1e-24, 1e-35, 1e-38], np.float32), (128, 4)).copy()
        m["maskbias"] = np.ascontiguousarray(mb.reshape(NT, 128).T)
        m["goff"] = np.full((128, 1), float(lo), np.float32)
        alpha = np.zeros((128, 8), np.float32)
        madd = np.full((128, 8), NEG, np.float32)
        for c in range(8):
            if c // 4 == b and c % 4 < qi:
                alpha[:, c] = (qi - c % 4) * OWN - 1
                madd[:, c] = 0.0
        m["alpha_in"] = alpha
        m["maskadd"] = madd
        sel = np.zeros((128, 8), np.float32)
        if qi > 0:
            sel[:, core - 1] = 1.0
        m["selx"] = sel
        in_maps.append(m)
    return in_maps


def kernel(**inp):
    in_maps = _prep(inp)
    nc = build()
    res = run_bass_kernel_spmd(nc, in_maps, core_ids=list(range(8)))
    out = np.zeros((B, T, D), np.float32)
    for core in range(8):
        b, qi = core // 4, core % 4
        out[b, qi * OWN:(qi + 1) * OWN] = res.results[core]["y"]
    return out
